# version 17
# baseline (speedup 1.0000x reference)
import math
import contextlib
import numpy as np

import concourse.bass as bass
import concourse.mybir as mybir
from concourse.bass_utils import run_bass_kernel_spmd

F32 = mybir.dt.float32
BF16 = mybir.dt.bfloat16
I32 = mybir.dt.int32
AF = mybir.ActivationFunctionType
ALU = mybir.AluOpType
AX = mybir.AxisListType

D = 1024
T = 2048
NCH = 8
NT4 = 4
NEG = -3.0e38
HEADS = 8
DH = 64
INNER = 512
HID = 2730
H2 = 5460
NH1 = 22
HIDPAD = NH1 * 128
NH_B = 16


def _hl(v):
    hi = np.float32(v)
    lo = np.float32(np.float64(v) - np.float64(hi))
    return float(hi), float(lo)


def cdiv(a, b):
    return -(-a // b)


NDMA = 8


class Builder:
    def __init__(self, nc, es):
        self.nc = nc
        self.es = es
        self.stack = [es]
        self.sems = {}
        self.cnt = {}
        self.waited = {}
        for name in ("t", "v", "s", "g"):
            self.sems[name] = es.enter_context(nc.semaphore("sem_" + name))
            self.cnt[name] = 0
        for q in ("dq", "dg"):
            for i in range(NDMA):
                nm = "%s%d" % (q, i)
                self.sems[nm] = es.enter_context(nc.semaphore("sem_" + nm))
                self.cnt[nm] = 0
        self.rr = {"dq": 0, "dg": 0}
        self.uid = 0

    def name(self, stem):
        self.uid += 1
        return "%s_%d" % (stem, self.uid)

    def sb(self, stem, shape, dtype):
        return self.stack[-1].enter_context(
            self.nc.sbuf_tensor(self.name(stem), shape, dtype))

    def ps(self, stem, shape):
        return self.stack[-1].enter_context(
            self.nc.psum_tensor(self.name(stem), shape, F32))

    @contextlib.contextmanager
    def phase(self):
        ps = contextlib.ExitStack()
        self.stack.append(ps)
        try:
            yield
        finally:
            self.barrier()
            self.stack.pop()
            ps.close()

    def _inc(self, instr, sem, n):
        self.cnt[sem] += n
        instr.then_inc(self.sems[sem], n)
        return (sem, self.cnt[sem])

    def _wait(self, eng, ename, deps):
        for tok in deps:
            if tok is None:
                continue
            sem, val = tok
            key = (ename, sem)
            if self.waited.get(key, 0) < val:
                eng.wait_ge(self.sems[sem], val)
                self.waited[key] = val

    def te(self, fn, deps=()):
        self._wait(self.nc.tensor, "t", deps)
        return self._inc(fn(self.nc.tensor), "t", 1)

    def ve(self, fn, deps=()):
        self._wait(self.nc.vector, "v", deps)
        return self._inc(fn(self.nc.vector), "v", 1)

    def se(self, fn, deps=()):
        self._wait(self.nc.scalar, "s", deps)
        return self._inc(fn(self.nc.scalar), "s", 1)

    def ge(self, fn, deps=()):
        self._wait(self.nc.gpsimd, "g", deps)
        return self._inc(fn(self.nc.gpsimd), "g", 1)

    def _dma(self, queue, eng, ename, start_fn, deps):
        i = self.rr[queue]
        self.rr[queue] = (i + 1) % NDMA
        sem = "%s%d" % (queue, i)
        if self.cnt[sem] > 0:
            self._wait(eng, ename, ((sem, self.cnt[sem]),))
        self._wait(eng, ename, deps)
        return self._inc(start_fn(), sem, 16)

    def dq(self, out, in_, deps=(), transpose=False, slow=False):
        kw = dict(transpose=transpose)
        if slow:
            kw["allow_slow_non_contiguous"] = True
        return self._dma("dq", self.nc.sync, "q",
                         lambda: self.nc.sync.dma_start(out, in_, **kw), deps)

    def dg(self, out, in_, deps=()):
        return self._dma("dg", self.nc.gpsimd, "g",
                         lambda: self.nc.gpsimd.dma_start(out, in_), deps)

    def mm(self, out, lhsT, rhs, start, stop, deps=(), tile_position=None):
        self._wait(self.nc.tensor, "t", deps)
        kw = {}
        if tile_position is not None:
            kw["tile_position"] = tile_position
        return self._inc(self.nc.tensor.matmul(out, lhsT, rhs, start=start,
                                               stop=stop, **kw), "t", 1)

    def barrier(self):
        nc = self.nc
        names = ["t", "v", "s", "g"] + ["%s%d" % (q, i) for q in ("dq", "dg")
                                        for i in range(NDMA)]
        toks = tuple((s, self.cnt[s]) for s in names)
        for eng, ename in ((nc.tensor, "t"), (nc.vector, "v"), (nc.scalar, "s"),
                           (nc.gpsimd, "g"), (nc.sync, "q")):
            self._wait(eng, ename, toks)


def build_tables(b, cT):
    tv = b.sb("tv", [128, 16], F32)
    d2 = b.sb("d2", [128, 32], F32)
    invf = b.sb("invf", [128, 32], F32)
    aux = b.sb("aux", [128, 32], F32)
    base = b.sb("base", [128, 32], F32)
    powr = b.sb("powr", [128, 16], F32)
    freqs = b.sb("freqs", [128, 512], F32)
    pw = b.sb("pw", [128, 512], F32)
    scl = b.sb("scl", [128, 512], F32)
    sclv = b.sb("sclv", [128, 512], F32)
    sinb = b.sb("sinb", [128, 512], F32)
    cosb = b.sb("cosb", [128, 512], F32)
    karg = b.sb("karg", [128, 512], F32)
    ktmp = b.sb("ktmp", [128, 512], F32)
    ktmp2 = b.sb("ktmp2", [128, 512], F32)
    ki = b.sb("ki", [128, 512], I32)
    kf = b.sb("kf", [128, 512], F32)
    sg = b.sb("sg", [128, 512], F32)

    t_tv = b.ge(lambda g: g.iota(tv[:], [[128, 16]], channel_multiplier=1,
                                 allow_small_or_imprecise_dtypes=True))
    t_d2 = b.ge(lambda g: g.iota(d2[:], [[2, 32]], channel_multiplier=0,
                                 allow_small_or_imprecise_dtypes=True))

    chi, clo = _hl(math.log(10000.0) / 64.0)
    t1 = b.ve(lambda v: v.tensor_scalar(invf[:], d2[:], -chi, None, ALU.mult),
              deps=(t_d2,))
    t2 = b.ve(lambda v: v.tensor_scalar(aux[:], d2[:], -clo, None, ALU.mult),
              deps=(t_d2,))
    t3 = b.ve(lambda v: v.tensor_tensor(invf[:], invf[:], aux[:], ALU.add),
              deps=(t1, t2))
    t4 = b.se(lambda s: s.activation(invf[:], invf[:], AF.Exp), deps=(t3,))

    t5 = b.ve(lambda v: v.tensor_tensor(
        freqs.ap().rearrange("p (a i) -> p a i", a=16),
        tv.ap().unsqueeze(2).to_broadcast([128, 16, 32]),
        invf.ap().unsqueeze(1).to_broadcast([128, 16, 32]),
        ALU.mult), deps=(t_tv, t4))

    ahi, alo = _hl(25.6)
    rhi, rlo = _hl(1.0 / 89.6)
    tb1 = b.ve(lambda v: v.tensor_scalar(base[:], d2[:], ahi, alo, ALU.add,
                                         ALU.add), deps=(t_d2,))
    tb2 = b.ve(lambda v: v.tensor_scalar(aux[:], base[:], rhi, None, ALU.mult),
               deps=(tb1,))
    tb3 = b.ve(lambda v: v.tensor_scalar(base[:], base[:], rlo, None, ALU.mult),
               deps=(tb2,))
    tb4 = b.ve(lambda v: v.tensor_tensor(base[:], base[:], aux[:], ALU.add),
               deps=(tb2, tb3))
    tb5 = b.se(lambda s: s.activation(base[:], base[:], AF.Ln), deps=(tb4,))
    tb6 = b.ve(lambda v: v.tensor_scalar(powr[:], tv[:], 1.0 / 256.0, 4.0,
                                         ALU.mult, ALU.subtract), deps=(t_tv,))
    tb7 = b.ve(lambda v: v.tensor_tensor(
        pw.ap().rearrange("p (a i) -> p a i", a=16),
        powr.ap().unsqueeze(2).to_broadcast([128, 16, 32]),
        base.ap().unsqueeze(1).to_broadcast([128, 16, 32]),
        ALU.mult), deps=(tb5, tb6))
    t_scl = b.se(lambda s: s.activation(scl[:], pw[:], AF.Exp), deps=(tb7,))
    t_sclv = b.se(lambda s: s.activation(sclv[:], pw[:], AF.Exp, scale=-1.0),
                  deps=(tb7,))

    p1, p2 = _hl(2.0 * math.pi)
    p3 = float(np.float32(np.float64(2.0 * math.pi) - np.float64(p1)
                          - np.float64(p2)))
    pih, pil = _hl(math.pi)
    inv2pi = float(np.float32(1.0 / (2.0 * math.pi)))
    THR = 3.1415925

    def sincos(shift_half_pi, out_sb, deps):
        if shift_half_pi:
            hh, hl2 = _hl(math.pi / 2.0)
            d1 = b.ve(lambda v: v.tensor_scalar(karg[:], freqs[:], hh, hl2,
                                                ALU.add, ALU.add), deps=deps)
        else:
            d1 = b.ve(lambda v: v.tensor_copy(karg[:], freqs[:]), deps=deps)
        d2_ = b.ve(lambda v: v.tensor_scalar(ktmp[:], karg[:], inv2pi, None,
                                             ALU.mult), deps=(d1,))
        d3 = b.ve(lambda v: v.tensor_copy(ki[:], ktmp[:]), deps=(d2_,))
        d4 = b.ve(lambda v: v.tensor_copy(kf[:], ki[:]), deps=(d3,))
        d5 = b.ve(lambda v: v.scalar_tensor_tensor(karg[:], kf[:], -p1, karg[:],
                                                   ALU.mult, ALU.add), deps=(d4,))
        d6 = b.ve(lambda v: v.scalar_tensor_tensor(karg[:], kf[:], -p2, karg[:],
                                                   ALU.mult, ALU.add), deps=(d5,))
        d7 = b.ve(lambda v: v.scalar_tensor_tensor(karg[:], kf[:], -p3, karg[:],
                                                   ALU.mult, ALU.add), deps=(d6,))
        d8 = b.ve(lambda v: v.tensor_scalar(ktmp[:], karg[:], THR, None,
                                            ALU.is_gt), deps=(d7,))
        d9 = b.ve(lambda v: v.tensor_scalar(ktmp2[:], karg[:], -THR, None,
                                            ALU.is_lt), deps=(d7,))
        d10 = b.ve(lambda v: v.tensor_tensor(ktmp[:], ktmp[:], ktmp2[:],
                                             ALU.subtract), deps=(d8, d9))
        d11 = b.ve(lambda v: v.scalar_tensor_tensor(karg[:], ktmp[:], -pih,
                                                    karg[:], ALU.mult, ALU.add),
                   deps=(d10,))
        d12 = b.ve(lambda v: v.scalar_tensor_tensor(karg[:], ktmp[:], -pil,
                                                    karg[:], ALU.mult, ALU.add),
                   deps=(d11,))
        d13 = b.ve(lambda v: v.tensor_tensor(ktmp2[:], ktmp[:], ktmp[:],
                                             ALU.mult), deps=(d12,))
        d14 = b.ve(lambda v: v.tensor_scalar(ktmp2[:], ktmp2[:], -2.0, 1.0,
                                             ALU.mult, ALU.add), deps=(d13,))
        d15 = b.se(lambda s: s.activation(sg[:], karg[:], AF.Sin), deps=(d12,))
        return b.ve(lambda v: v.tensor_tensor(out_sb[:], sg[:], ktmp2[:],
                                              ALU.mult), deps=(d14, d15))

    t_sin = sincos(False, sinb, (t5,))
    t_cos = sincos(True, cosb, (t5, t_sin))

    toks = []
    for nm, trig, trig_tok, sc, sc_tok, mulc in (
            ("c1q", cosb, t_cos, scl, t_scl, 8.0),
            ("c2q", sinb, t_sin, scl, t_scl, 8.0),
            ("c1k", cosb, t_cos, sclv, t_sclv, 1.0),
            ("c2k", sinb, t_sin, sclv, t_sclv, 1.0)):
        dst = cT[nm]
        toks.append(b.ve(lambda v, a=trig, c=sc, o=dst, m=mulc:
                         v.scalar_tensor_tensor(o[:], a[:], m, c[:],
                                                ALU.mult, ALU.mult),
                         deps=(trig_tok, sc_tok, t_cos)))
    return toks


DBG = {}


def build_program(stages=4):
    nc = bass.Bass("TRN2", target_bir_lowering=False,
                   detect_race_conditions=bool(DBG.get("race")))
    es = contextlib.ExitStack()
    b = Builder(nc, es)

    dr = {}

    def din(name, shape, dtype=F32):
        dr[name] = nc.dram_tensor(name, shape, dtype, kind="ExternalInput").ap()
        return dr[name]

    x_in = din("x_fm", [D, T])
    for nm in ("aWqT", "aWkT", "aWvT", "bWqT", "bWkT", "bWvT", "bWoT"):
        din(nm, [D, D])
    for nm in ("a_bq", "a_bk", "a_bv", "b_bq", "b_bk", "b_bv", "b_bo"):
        din(nm, [D])
    din("f_wqkv", [2, D, 3 * INNER])
    din("f_wo", [2, INNER, D])
    din("f_w1", [2, D, H2])
    din("f_w2p", [2, HIDPAD, D])
    din("f_qs", [2, DH])
    din("f_ks", [2, DH])
    din("f_qs_rot", [2, DH])
    din("f_ks_rot", [2, DH])
    din("f_lng", [2, D])
    din("f_lnb", [2, D])
    din("f_b1", [2, H2])
    din("f_b2", [2, D])
    din("m_diag", [128, 128])
    din("m_diag_t", [128, 128])
    din("m_low_t", [128, 128])
    din("m_up_t", [128, 128])
    din("ident128", [128, 128])
    din("m_low", [128, 128])
    din("m_up", [128, 128])
    din("e8", [8, 512])

    x_out = nc.dram_tensor("x_outf", [D, T], F32, kind="ExternalOutput").ap()
    x_A = nc.dram_tensor("x_A", [D, T], F32).ap()
    x_B = nc.dram_tensor("x_B", [D, T], F32).ap()
    f_A = nc.dram_tensor("f_A", [D, T], F32).ap()
    f_B = nc.dram_tensor("f_B", [D, T], F32).ap()
    ao_d = nc.dram_tensor("ao_d", [D, T], BF16).ap()
    tap_ag = nc.dram_tensor("tap_ag", [HIDPAD, T], BF16).ap()

    cT = {}
    cT["ones_bf"] = b.sb("ones_bf", [128, 1], BF16)
    cT["ones_f"] = b.sb("ones_f", [1, 128], F32)
    c_mdiag = b.sb("c_mdiag", [128, 128], F32)
    c_mdT_bf = b.sb("c_mdT_bf", [128, 128], BF16)
    c_mlT_bf = b.sb("c_mlT_bf", [128, 128], BF16)
    c_muT_bf = b.sb("c_muT_bf", [128, 128], BF16)
    c_id_bf = b.sb("c_id_bf", [128, 128], BF16)
    c_mlow = b.sb("c_mlow", [128, 128], F32)
    c_mup = b.sb("c_mup", [128, 128], F32)
    c_e8 = b.sb("c_e8", [8, 512], F32)

    cT["eps5"] = b.sb("c_eps5", [128, 1], F32)
    cT["eps30"] = b.sb("c_eps30", [128, 1], F32)
    init = []
    init.append(b.ve(lambda v: v.memset(cT["eps5"][:], 1.0e-5)))
    init.append(b.ve(lambda v: v.memset(cT["eps30"][:], 1.0e-30)))
    init.append(b.ve(lambda v: v.memset(cT["ones_bf"][:], 1.0)))
    init.append(b.ve(lambda v: v.memset(cT["ones_f"][:], 1.0)))
    init.append(b.dq(c_mdiag[:], dr["m_diag"]))
    init.append(b.dg(c_mdT_bf[:], dr["m_diag_t"]))
    init.append(b.dg(c_mlT_bf[:], dr["m_low_t"]))
    init.append(b.dg(c_muT_bf[:], dr["m_up_t"]))
    init.append(b.dg(c_id_bf[:], dr["ident128"]))
    init.append(b.dq(c_mlow[:], dr["m_low"]))
    init.append(b.dq(c_mup[:], dr["m_up"]))
    init.append(b.dq(c_e8[:], dr["e8"]))
    for nm in ("c1q", "c2q", "c1k", "c2k"):
        cT[nm] = b.sb(nm, [128, 512], F32)
    with b.phase():
        build_tables(b, cT)

    def emit_ln_core(x_loader, out_writer, gain_bias=None):
        xf = b.sb("ln_x", [128, 2 * T], F32)
        xb = b.sb("ln_xb", [128, 2 * T], BF16)
        xq = b.sb("ln_xq", [128, T], BF16)
        srow = b.sb("ln_srow", [1, 2 * T], F32)
        vrow = b.sb("ln_vrow", [1, T], F32)
        nrow = srow[:, 0:T]
        ps_es = contextlib.ExitStack()
        ps_s = ps_es.enter_context(
            b.nc.psum_tensor(b.name("ln_ps"), [1, 2 * T], F32))
        mm_tok = None
        prev_use = [None, None]
        prev_mm = [None, None]
        for c in range(NCH):
            sl = c % 2
            o = sl * T
            ld = x_loader(c, xf[:, o:o + T], prev_use[sl])
            tc = b.se(lambda s, o=o: s.activation(xb[:, o:o + T], xf[:, o:o + T],
                                                  AF.Identity),
                      deps=(ld, prev_mm[sl]))
            tq = b.se(lambda s, o=o: s.activation(xq[:], xf[:, o:o + T],
                                                  AF.Square),
                      deps=(ld, mm_tok))
            prev_use[sl] = tq
            for h in range(NT4):
                mm_tok = b.mm(ps_s[:, h * 512:(h + 1) * 512], cT["ones_bf"][:],
                              xb[:, o + h * 512:o + (h + 1) * 512],
                              start=(c == 0), stop=(c == NCH - 1),
                              deps=(tc, mm_tok))
                mm_tok = b.mm(ps_s[:, T + h * 512:T + (h + 1) * 512],
                              cT["ones_bf"][:], xq[:, h * 512:(h + 1) * 512],
                              start=(c == 0), stop=(c == NCH - 1), deps=(tq,))
            prev_mm[sl] = mm_tok
        ev = b.ve(lambda v: v.tensor_copy(srow[:], ps_s[:]), deps=(mm_tok,))
        ps_es.close()
        t_m2 = b.ve(lambda v: v.scalar_tensor_tensor(
            vrow[:], srow[:, 0:T], 1.0 / D, srow[:, 0:T], ALU.mult, ALU.mult),
            deps=(ev,))
        t_m3 = b.ve(lambda v: v.tensor_scalar(vrow[:], vrow[:], 1.0 / D, None,
                                              ALU.mult), deps=(t_m2,))
        t_var = b.ve(lambda v: v.scalar_tensor_tensor(
            vrow[:], srow[:, T:2 * T], 1.0 / D, vrow[:], ALU.mult, ALU.subtract),
            deps=(t_m3,))
        t_ln = b.se(lambda s: s.activation(vrow[:], vrow[:], AF.Ln,
                                           bias=cT["eps5"][0:1, :]),
                    deps=(t_var,) + tuple(init))
        t_r = b.se(lambda s: s.activation(vrow[:], vrow[:], AF.Exp, scale=-0.5),
                   deps=(t_ln,))
        t_nm = b.ve(lambda v: v.tensor_scalar(nrow, srow[:, 0:T], -1.0 / D, None,
                                              ALU.mult), deps=(t_m2, t_var))
        ps_es2 = contextlib.ExitStack()
        ps_rep = ps_es2.enter_context(
            b.nc.psum_tensor(b.name("ln_rep"), [128, 2 * T], F32))
        reps = []
        for h in range(NT4):
            reps.append(b.mm(ps_rep[:, h * 512:(h + 1) * 512], cT["ones_f"][:],
                             vrow[:, h * 512:(h + 1) * 512], start=True,
                             stop=True, deps=(t_r,)))
            reps.append(b.mm(ps_rep[:, T + h * 512:T + (h + 1) * 512],
                             cT["ones_f"][:], nrow[0:1, h * 512:(h + 1) * 512],
                             start=True, stop=True, deps=(t_nm,)))
        utmp = b.sb("ln_ut", [128, T], F32)
        outs = []
        prev_wr = None
        prev_use2 = [mm_tok, mm_tok]
        for c in range(NCH):
            sl = c % 2
            o = sl * T
            ld = x_loader(c, xf[:, o:o + T], prev_use2[sl])
            t_u = b.ve(lambda v, o=o: v.tensor_tensor(
                utmp[:], xf[:, o:o + T], ps_rep[:, T:2 * T], ALU.add),
                deps=tuple(reps) + (ld, prev_wr))
            prev_use2[sl] = t_u
            t_u2 = b.ve(lambda v: v.tensor_tensor(
                utmp[:], utmp[:], ps_rep[:, 0:T], ALU.mult),
                deps=(t_u,))
            if gain_bias is not None:
                g_sb, b_sb = gain_bias
                t_u2 = b.ve(lambda v, c=c: v.tensor_scalar(
                    utmp[:], utmp[:], g_sb[:, c:c + 1],
                    b_sb[:, c:c + 1], ALU.mult, ALU.add), deps=(t_u2,))
            prev_wr = out_writer(c, utmp[:], (t_u2,))
            outs.append(prev_wr)
        ps_es2.close()
        return outs

    def ln_to_sbuf_bf(x_dram, out_bf, gain_bias=None, shift=False):
        def loader(c, dst, war_dep):
            if not shift or c < NCH // 2:
                return b.dq(dst, x_dram[c * 128:(c + 1) * 128, :],
                            deps=(war_dep,))
            tz = b.ve(lambda v: v.memset(dst[:, 0:1], 0.0), deps=(war_dep,))
            return b.dq(dst[:, 1:T], x_dram[c * 128:(c + 1) * 128, 0:T - 1],
                        deps=(tz,))

        def writer(c, utmp_ap, deps):
            return b.ve(lambda v: v.tensor_copy(out_bf[:, c * T:(c + 1) * T],
                                                utmp_ap), deps=deps)
        return emit_ln_core(loader, writer, gain_bias)

    def ln_to_dram_f32(x_dram, out_dram):
        def loader(c, dst, war_dep):
            return b.dq(dst, x_dram[c * 128:(c + 1) * 128, :], deps=(war_dep,))

        def writer(c, utmp_ap, deps):
            return b.dq(out_dram[c * 128:(c + 1) * 128, :], utmp_ap, deps=deps)
        return emit_ln_core(loader, writer, None)

    def project_fm(wT_dram, bias_dram, src_bf, out_bf):
        wbuf = b.sb("prj_w", [128, 2 * D], BF16)
        bsb = b.sb("prj_b", [128, NCH], F32)
        tb = b.dq(bsb.ap(), bias_dram.rearrange("(a p) -> p a", p=128), slow=True)
        ps = [b.ps("prj_ps", [128, 512]) for _ in range(4)]
        toks = []
        wtok = [None, None]
        for e in range(NCH):
            slot = e % 2
            wtok[slot] = b.dg(
                wbuf[:, slot * D:(slot + 1) * D].rearrange(
                    "p (c e) -> p c e", c=NCH),
                wT_dram[:, e * 128:(e + 1) * 128].rearrange(
                    "(c p) e -> p c e", p=128),
                deps=(toks[4 * (e - 2) + 3] if e >= 2 else None,))
            for h in range(NT4):
                pst = ps[(e * NT4 + h) % 4]
                mt = None
                for c in range(NCH):
                    mt = b.mm(pst[:],
                              wbuf[:, slot * D + c * 128: slot * D + (c + 1) * 128],
                              src_bf[:, c * T + h * 512: c * T + h * 512 + 512],
                              start=(c == 0), stop=(c == NCH - 1),
                              deps=(wtok[slot],
                                    toks[4 * e + h - 4] if 4 * e + h >= 4
                                    else None))
                toks.append(b.se(
                    lambda s, e=e, h=h, pst=pst: s.activation(
                        out_bf[:, e * T + h * 512: e * T + h * 512 + 512],
                        pst[:], AF.Identity, bias=bsb[:, e:e + 1]),
                    deps=(mt, tb)))
        return toks

    def project_tm(wT_dram, bias_dram, src_bf, out_tm):
        wbuf = b.sb("pv_w", [128, 2 * NCH * 512], BF16)
        brow = b.sb("pv_brow", [1, D], F32)
        brep = b.sb("pv_brepsb", [128, D], F32)
        ps_brep = b.ps("pv_brep", [128, D])
        tb = b.dq(brow[:], bias_dram.unsqueeze(0))
        btoks = [b.mm(ps_brep[:, h * 512:(h + 1) * 512], cT["ones_f"][:],
                      brow[:, h * 512:(h + 1) * 512], start=True, stop=True,
                      deps=(tb,)) for h in range(2)]
        btoks = [b.se(lambda s: s.activation(brep[:], ps_brep[:], AF.Identity),
                      deps=tuple(btoks))]
        ps = [b.ps("pv_ps", [128, 512]) for _ in range(4)]
        toks = []
        wtok = [None, None]
        for e2 in range(2):
            slot = e2 % 2
            wtok[slot] = b.dg(
                wbuf[:, slot * NCH * 512:(slot + 1) * NCH * 512].rearrange(
                    "p (c e) -> p c e", c=NCH),
                wT_dram[:, e2 * 512:(e2 + 1) * 512].rearrange(
                    "(c p) e -> p c e", p=128),
                deps=(toks[-8] if len(toks) >= 8 else None,))
            for tc in range(16):
                pst = ps[tc % 4]
                mt = None
                for c in range(NCH):
                    mt = b.mm(pst[:],
                              src_bf[:, c * T + tc * 128: c * T + (tc + 1) * 128],
                              wbuf[:, slot * NCH * 512 + c * 512:
                                   slot * NCH * 512 + (c + 1) * 512],
                              start=(c == 0), stop=(c == NCH - 1),
                              deps=(wtok[slot],
                                    toks[-4] if len(toks) >= 4 else None))
                toks.append(b.ve(
                    lambda v, tc=tc, e2=e2, pst=pst: v.tensor_tensor(
                        out_tm[:, tc * D + e2 * 512: tc * D + (e2 + 1) * 512],
                        pst[:], brep[:, e2 * 512:(e2 + 1) * 512], ALU.add),
                    deps=(mt,) + tuple(btoks)))
        return toks

    def attn_causal(kind, x_src, x_dst):
        pre = kind
        with b.phase():
            Qf = b.sb(pre + "_Q", [128, NCH * T], BF16)
            Kf = b.sb(pre + "_K", [128, NCH * T], BF16)
            Vt = b.sb(pre + "_V", [128, 16 * D], BF16)
            with b.phase():
                xhat = b.sb(pre + "_xhat", [128, NCH * T], BF16)
                with b.phase():
                    ln_to_sbuf_bf(x_src, xhat)
                with b.phase():
                    project_fm(dr[pre + "WqT"], dr[pre + "_bq"], xhat, Qf)
                with b.phase():
                    project_fm(dr[pre + "WkT"], dr[pre + "_bk"], xhat, Kf)
                with b.phase():
                    project_tm(dr[pre + "WvT"], dr[pre + "_bv"], xhat, Vt)
            if kind == "b":
                aout = b.sb("b_ao", [128, 16 * D], BF16)

            iscale = 1.0 / 32.0 if kind == "a" else 1.0 / 8.0
            xtile = b.sb(pre + "_xt", [128, 2 * 512], F32)
            rtile = b.sb(pre + "_rt", [128, 2 * 512], F32)
            att_ps = contextlib.ExitStack()

            def asb(stem, shape, dtype):
                return att_ps.enter_context(
                    b.nc.sbuf_tensor(b.name(stem), shape, dtype))

            if kind == "a":
                psA = [att_ps.enter_context(b.nc.psum_tensor(
                    b.name(pre + "_S"), [128, 512], F32)) for _ in range(3)]
                psA_tok = [None, None, None]
                psA_rr = [0]
                S_sb = [asb(pre + "_Ssb", [128, T], F32) for _ in range(2)]
            else:
                psS = [att_ps.enter_context(b.nc.psum_tensor(
                    b.name(pre + "_Sh"), [128, 1024], F32)) for _ in range(3)]
                psS_tok = [None, None, None]
                psS_rr = [0]
                S_sb = [asb(pre + "_Ssb", [128, T], F32) for _ in range(2)]
                R_sb = asb(pre + "_R", [128, T], F32)
            t32 = [asb(pre + "_t32", [128, 32], F32) for _ in range(2)]
            nmax = [asb(pre + "_nm", [128, 1], F32) for _ in range(2)]
            zcol = [asb(pre + "_z", [128, 1], F32) for _ in range(2)]
            rcol = [asb(pre + "_r", [128, 1], F32) for _ in range(2)]
            lcol = [asb(pre + "_l", [128, 1], F32) for _ in range(2)]
            Pb = [asb(pre + "_P", [128, T], BF16) for _ in range(2)]
            Pm = [asb(pre + "_Pm", [128, T], BF16) for _ in range(2)]
            PT = asb(pre + "_PT", [128, 16 * 512], BF16)
            ps_av = [att_ps.enter_context(b.nc.psum_tensor(
                b.name(pre + "_av"), [128, 512], F32)) for _ in range(2)]

            prev_sel = [None]
            prev_tr = [None]
            pS_read = [None, None]
            pS_exp = [None, None]
            pPm_tr = [None, None]
            pZrd = [None, None]
            prev_pm = [None]
            prev_exp = [None]

            pNm = [None, None]
            pZv = [None, None]
            pPb_rd = [None, None]

            def head_a(qt):
                W = 128 * (qt + 1)
                p = qt % 2
                ev = None
                for w0 in range(0, W, 512):
                    w1 = min(W, w0 + 512)
                    last = (w1 == W)
                    sl = psA_rr[0] % 3
                    psA_rr[0] += 1
                    mt = None
                    for c in range(NCH):
                        mt = b.mm(psA[sl][:, 0:w1 - w0],
                                  Qf[:, c * T + qt * 128:
                                     c * T + (qt + 1) * 128],
                                  Kf[:, c * T + w0: c * T + w1],
                                  start=(c == 0),
                                  stop=(c == NCH - 1 and not last),
                                  deps=(psA_tok[sl],))
                    if last:
                        mt = b.mm(psA[sl][:, W - 128 - w0:W - w0],
                                  c_mdT_bf[:], c_id_bf[:],
                                  start=False, stop=True, deps=(mt,),
                                  tile_position=(0, 0))
                    ev = b.se(lambda s, sl=sl, w0=w0, w1=w1, p=p:
                              s.activation(S_sb[p][:, w0:w1],
                                           psA[sl][:, 0:w1 - w0],
                                           AF.Identity),
                              deps=(mt, pS_read[p], ev))
                    psA_tok[sl] = ev
                return ev

            def tail_a_A(qt, ev):
                W = 128 * (qt + 1)
                p = qt % 2
                t_max = b.ve(lambda v, W=W, p=p: v.max(
                    out=t32[p][:, 0:8], in_=S_sb[p][:, 0:W]),
                    deps=(ev, pNm[p]) + tuple(init))
                t_nm = b.se(lambda s, p=p: s.activation(
                    nmax[p][:], t32[p][:, 0:1], AF.Identity, scale=-iscale),
                    deps=(t_max, pS_exp[p]))
                pNm[p] = t_nm
                t_exp = b.se(lambda s, W=W, p=p: s.activation(
                    Pb[p][:, 0:W], S_sb[p][:, 0:W], AF.Exp, bias=nmax[p][:],
                    scale=iscale, accum_out=zcol[p][:]),
                    deps=(t_nm, pPb_rd[p], pZv[p]))
                pS_exp[p] = t_exp
                pS_read[p] = t_exp
                return t_exp

            def tail_a_B(qt, qslot, t_exp):
                W = 128 * (qt + 1)
                p = qt % 2
                t_r = b.ve(lambda v, p=p: v.reciprocal(rcol[p][:], zcol[p][:]),
                           deps=(t_exp,))
                pZv[p] = t_r
                t_pm = b.ve(lambda v, W=W, p=p: v.tensor_scalar(
                    Pm[p][:, 0:W], Pb[p][:, 0:W], rcol[p][:], None, ALU.mult),
                    deps=(t_r, pPm_tr[p]))
                pPb_rd[p] = t_pm
                t_tr = b.dq(PT.ap().rearrange("p (c j) -> p c j", j=512)
                            [:, 0:W // 128, qslot * 128:(qslot + 1) * 128],
                            Pm[p][:, 0:W], deps=(t_pm, prev_tr[0]),
                            transpose=True)
                pPm_tr[p] = t_tr
                return t_tr

            def head_b(h, qt):
                W = 128 * (qt + 1)
                p = qt % 2
                hb, hoff = h // 2, 64 * (h % 2)
                ev = None
                for a0 in range(0, W, 1024):
                    a1 = min(W, a0 + 1024)
                    sl = psS_rr[0] % 3
                    psS_rr[0] += 1
                    mt = None
                    for w0 in range(a0, a1, 512):
                        w1 = min(a1, w0 + 512)
                        last = (w1 == W)
                        mt = b.mm(psS[sl][:, w0 - a0:w1 - a0],
                                  Qf[:, hb * T + qt * 128:
                                     hb * T + (qt + 1) * 128][hoff:hoff + 64, :],
                                  Kf[:, hb * T + w0: hb * T + w1]
                                  [hoff:hoff + 64, :],
                                  start=True, stop=not last,
                                  deps=(psS_tok[sl],))
                        if last:
                            mt = b.mm(psS[sl][:, W - 128 - a0:W - a0],
                                      c_mdT_bf[:], c_id_bf[:],
                                      start=False, stop=True, deps=(mt,),
                                      tile_position=(0, 0))
                    ev = b.se(lambda s, sl=sl, a0=a0, a1=a1, p=p: s.activation(
                        S_sb[p][:, a0:a1], psS[sl][:, 0:a1 - a0], AF.Identity),
                        deps=(mt, pS_read[p], ev))
                    psS_tok[sl] = ev
                return ev

            pNm = [None, None]

            def tail_b(h, qt, qslot, ev):
                W = 128 * (qt + 1)
                p = qt % 2
                r1 = b.ve(lambda v, W=W, p=p: v.max(out=t32[p][:, 0:8],
                                                    in_=S_sb[p][:, 0:W]),
                          deps=(ev, pNm[p]) + tuple(init))
                t_nm = b.se(lambda s, p=p: s.activation(
                    nmax[p][:], t32[p][:, 0:1], AF.Identity, scale=-iscale),
                    deps=(r1, pS_exp[p]))
                pNm[p] = t_nm
                t_exp = b.se(lambda s, W=W, p=p: s.activation(
                    Pb[p][:, 0:W], S_sb[p][:, 0:W], AF.Exp, bias=nmax[p][:],
                    scale=iscale), deps=(t_nm,))
                pS_exp[p] = t_exp
                tsel = r1
                for r in range(1, 4):
                    tsel = b.ve(lambda v, W=W, p=p, r=r: v.match_replace(
                        out=R_sb[:, 0:W],
                        in_to_replace=t32[p][:, 8 * (r - 1):8 * r],
                        in_values=(S_sb[p] if r == 1 else R_sb)[:, 0:W],
                        imm_value=NEG), deps=(tsel,))
                    tsel = b.ve(lambda v, W=W, p=p, r=r: v.max(
                        out=t32[p][:, 8 * r:8 * r + 8], in_=R_sb[:, 0:W]),
                        deps=(tsel,))
                t_stt = b.ve(lambda v, W=W, p=p: v.scalar_tensor_tensor(
                    Pm[p][:, 0:W], S_sb[p][:, 0:W], t32[p][:, 31:32],
                    Pb[p][:, 0:W], ALU.is_ge, ALU.mult, accum_out=zcol[p][:]),
                    deps=(t_exp, tsel, pPm_tr[p]))
                pS_read[p] = t_stt
                t_r = b.ve(lambda v, p=p: v.reciprocal(rcol[p][:], zcol[p][:]),
                           deps=(t_stt,))
                t_pm = b.ve(lambda v, W=W, p=p: v.tensor_scalar(
                    Pm[p][:, 0:W], Pm[p][:, 0:W], rcol[p][:], None, ALU.mult),
                    deps=(t_r,))
                t_tr = b.dq(PT.ap().rearrange("p (c j) -> p c j", j=512)
                            [:, 0:W // 128, qslot * 128:(qslot + 1) * 128],
                            Pm[p][:, 0:W], deps=(t_pm, prev_tr[0]),
                            transpose=True)
                pPm_tr[p] = t_tr
                return t_tr

            res_toks = []
            pend_evac = [None]

            def emit_pend():
                if pend_evac[0] is not None:
                    pend_evac[0]()
                    pend_evac[0] = None

            def flush_av(h, qb, tr_toks):
                kcmax = 4 * (qb + 1)
                if kind == "a":
                    last = None
                    for c in range(NCH):
                        pst = ps_av[c % 2]
                        tsl = c % 2
                        xt = xtile[:, tsl * 512:(tsl + 1) * 512]
                        rt = rtile[:, tsl * 512:(tsl + 1) * 512]
                        mt = None
                        for kc in range(kcmax):
                            mt = b.mm(
                                pst[:],
                                Vt.ap().rearrange("p (tc e) -> p tc e", tc=16)
                                [:, kc, c * 128:(c + 1) * 128],
                                PT[:, kc * 512:(kc + 1) * 512],
                                start=(kc == 0), stop=(kc == kcmax - 1),
                                deps=tuple(tr_toks) +
                                (res_toks[-2] if len(res_toks) >= 2 else None,))
                        lx = b.dq(xt,
                                  x_src[c * 128:(c + 1) * 128,
                                        qb * 512:(qb + 1) * 512],
                                  deps=(res_toks[-2] if len(res_toks) >= 2
                                        else None,))
                        ta = b.ve(lambda v, pst=pst, xt=xt, rt=rt:
                                  v.tensor_tensor(rt, pst[:], xt, ALU.add),
                            deps=(mt, lx,
                                  res_toks[-2] if len(res_toks) >= 2 else None))
                        st = b.dq(x_dst[c * 128:(c + 1) * 128,
                                        qb * 512:(qb + 1) * 512],
                                  rt, deps=(ta,))
                        res_toks.append(st)
                        last = mt
                    return last
                hb, hoff = h // 2, 64 * (h % 2)
                pst = ps_av[(h * 4 + qb) % 2]
                mt = None
                for kc in range(kcmax):
                    mt = b.mm(pst[hoff:hoff + 64, :],
                              Vt.ap().rearrange("p (tc e) -> p tc e", tc=16)
                              [:, kc, h * DH:(h + 1) * DH],
                              PT[:, kc * 512:(kc + 1) * 512],
                              start=(kc == 0), stop=(kc == kcmax - 1),
                              deps=tuple(tr_toks) +
                              (res_toks[-2] if len(res_toks) >= 2 else None,),
                              tile_position=(0, hoff))

                def mk_evac(hb=hb, hoff=hoff, qb=qb, pst=pst, mt=mt):
                    tk = b.se(lambda s: s.activation(
                        aout[:, hb * T + qb * 512:hb * T + (qb + 1) * 512]
                        [hoff:hoff + 64, :],
                        pst[hoff:hoff + 64, :], AF.Identity),
                        deps=(mt, res_toks[-1] if res_toks else None))
                    res_toks.append(tk)
                pend_evac[0] = mk_evac
                return mt

            def pt_fill(qb):
                zs = []
                for j in range(4):
                    qt = qb * 4 + j
                    for kc in range(qt + 1, 4 * (qb + 1)):
                        zs.append(b.ge(
                            lambda g, kc=kc, j=j: g.memset(
                                PT[:, kc * 512 + j * 128:
                                   kc * 512 + (j + 1) * 128], 0.0),
                            deps=(prev_tr[0],)))
                return zs

            if kind == "a":
                trs_map = {}
                ev_next = head_a(0)
                pB = None
                for qt in range(16):
                    qb, j = qt // 4, qt % 4
                    ev = ev_next
                    if qt + 1 < 16:
                        ev_next = head_a(qt + 1)
                    if j == 0:
                        trs_map[qb] = pt_fill(qb)
                    te = tail_a_A(qt, ev)
                    if pB is not None:
                        pqt, pte = pB
                        tr = tail_a_B(pqt, pqt % 4, pte)
                        trs_map[pqt // 4].append(tr)
                        if pqt % 4 == 3:
                            prev_tr[0] = flush_av(0, pqt // 4,
                                                  trs_map.pop(pqt // 4))
                    pB = (qt, te)
                pqt, pte = pB
                tr = tail_a_B(pqt, pqt % 4, pte)
                trs_map[pqt // 4].append(tr)
                prev_tr[0] = flush_av(0, pqt // 4, trs_map.pop(pqt // 4))
                b.barrier()
                att_ps.close()
            else:
                units = [(h, qb, j) for h in range(NH_B) for qb in range(4)
                         for j in range(4)]
                trs_map = {}
                ev_next = head_b(units[0][0], units[0][1] * 4 + units[0][2])
                for n, (h, qb, j) in enumerate(units):
                    ev = ev_next
                    if n + 1 < len(units):
                        nh_, nqb_, nj_ = units[n + 1]
                        ev_next = head_b(nh_, nqb_ * 4 + nj_)
                    if j == 0:
                        trs_map[(h, qb)] = pt_fill(qb)
                    tr = tail_b(h, qb * 4 + j, j, ev)
                    trs_map[(h, qb)].append(tr)
                    emit_pend()
                    if j == 3:
                        prev_tr[0] = flush_av(h, qb, trs_map.pop((h, qb)))
                emit_pend()
            b.barrier()
            att_ps.close()

            if kind == "b":
                wbuf = b.sb("bo_w", [128, 2 * D], BF16)
                bsb = b.sb("bo_b", [128, NCH], F32)
                tb = b.dq(bsb.ap(), dr["b_bo"].rearrange("(a p) -> p a", p=128), slow=True)
                ps2 = [b.ps("bo_ps", [128, 512]) for _ in range(2)]
                toks2 = []
                wtok = [None, None]
                for e in range(NCH):
                    slot = e % 2
                    wtok[slot] = b.dg(
                        wbuf[:, slot * D:(slot + 1) * D].rearrange(
                            "p (c e) -> p c e", c=NCH),
                        dr["bWoT"][:, e * 128:(e + 1) * 128].rearrange(
                            "(c p) e -> p c e", p=128),
                        deps=(toks2[4 * (e - 2) + 3] if e >= 2 else None,))
                    for h in range(NT4):
                        pst = ps2[(e * NT4 + h) % 2]
                        mt = None
                        for c in range(NCH):
                            mt = b.mm(
                                pst[:],
                                wbuf[:, slot * D + c * 128:
                                     slot * D + (c + 1) * 128],
                                aout[:, c * T + h * 512: c * T + h * 512 + 512],
                                start=(c == 0), stop=(c == NCH - 1),
                                deps=(wtok[slot],
                                      toks2[-2] if len(toks2) >= 2 else None))
                        tsl = (e * NT4 + h) % 2
                        xt = xtile[:, tsl * 512:(tsl + 1) * 512]
                        rt = rtile[:, tsl * 512:(tsl + 1) * 512]
                        lx = b.dq(xt,
                                  x_src[e * 128:(e + 1) * 128,
                                        h * 512:(h + 1) * 512],
                                  deps=(toks2[-2] if len(toks2) >= 2 else None,))
                        ta = b.ve(lambda v, pst=pst, e=e, xt=xt, rt=rt:
                                  v.scalar_tensor_tensor(
                                      rt, pst[:], bsb[:, e:e + 1], xt,
                                      ALU.add, ALU.add),
                            deps=(mt, lx, tb,
                                  toks2[-2] if len(toks2) >= 2 else None))
                        toks2.append(b.dq(
                            x_dst[e * 128:(e + 1) * 128, h * 512:(h + 1) * 512],
                            rt, deps=(ta,)))

    def ffn_inner(lidx, f_src, f_mid, dst_pair, stop_after_c=False):
        with b.phase():
            qf_bf = b.sb("fl_qf", [128, 4 * T], BF16)
            kf_bf = b.sb("fl_kf", [128, 4 * T], BF16)
            v_tm = b.sb("fl_vtm", [128, 16 * INNER], BF16)
            with b.phase():
                wqkv = b.sb("fl_wqkv", [128, NCH * 3 * INNER], BF16)
                w_toks = [b.dg(wqkv[:, c * 3 * INNER:(c + 1) * 3 * INNER],
                               dr["f_wqkv"][lidx, c * 128:(c + 1) * 128, :])
                          for c in range(NCH)]
                hbf = b.sb("fl_h", [128, NCH * T], BF16)
                h_toks = []
                for c in range(NCH):
                    if c < NCH // 2:
                        h_toks.append(b.dg(hbf[:, c * T:(c + 1) * T],
                                           f_src[c * 128:(c + 1) * 128, :]))
                    else:
                        t0 = b.ve(lambda v, c=c: v.memset(
                            hbf[:, c * T:c * T + 1], 0.0))
                        h_toks.append(b.dg(hbf[:, c * T + 1:(c + 1) * T],
                                           f_src[c * 128:(c + 1) * 128, 0:T - 1],
                                           deps=(t0,)))
                qs_rep = b.sb("fl_qsrep", [128, 4 * DH], F32)
                qrow = b.sb("fl_qrow", [1, 4 * DH], F32)
                tl = [b.dq(qrow[:, 0:DH], dr["f_qs"][lidx].unsqueeze(0)),
                      b.dq(qrow[:, DH:2 * DH], dr["f_qs_rot"][lidx].unsqueeze(0)),
                      b.dq(qrow[:, 2 * DH:3 * DH], dr["f_ks"][lidx].unsqueeze(0)),
                      b.dq(qrow[:, 3 * DH:4 * DH],
                           dr["f_ks_rot"][lidx].unsqueeze(0))]
                ps_qs = b.ps("fl_qsps", [128, 4 * DH])
                t_qs = b.mm(ps_qs[:], cT["ones_f"][:], qrow[:], start=True,
                            stop=True, deps=tuple(tl))
                t_qs2 = b.ve(lambda v: v.tensor_copy(qs_rep[:], ps_qs[:]),
                             deps=(t_qs,))
                Dt = {}
                d_toks = []
                for pr, c1, c2, off in (("q", "c1q", "c2q", 0),
                                        ("k", "c1k", "c2k", 2 * DH)):
                    D1 = b.sb("fl_D1" + pr, [128, 16 * DH], F32)
                    D2 = b.sb("fl_D2" + pr, [128, 16 * DH], F32)
                    Dt[pr] = (D1, D2)
                    d_toks.append(b.ve(lambda v, c1=c1, D1=D1, off=off:
                                       v.tensor_tensor(
                        D1.ap().rearrange("p (a x i) -> p a x i", a=16, x=2),
                        cT[c1].ap().rearrange("p (a i) -> p a i", a=16)
                        .unsqueeze(2).to_broadcast([128, 16, 2, 32]),
                        qs_rep[:, off:off + DH]
                        .rearrange("p (x i) -> p x i", x=2)
                        .unsqueeze(1).to_broadcast([128, 16, 2, 32]),
                        ALU.mult), deps=(t_qs2,)))
                    d_toks.append(b.ve(lambda v, c2=c2, D2=D2, off=off:
                                       v.scalar_tensor_tensor(
                        D2.ap().rearrange("p (a x i) -> p a x i", a=16, x=2)
                        [:, :, 0, :],
                        cT[c2].ap().rearrange("p (a i) -> p a i", a=16), -1.0,
                        qs_rep[:, off + DH:off + DH + 32].unsqueeze(1)
                        .to_broadcast([128, 16, 32]),
                        ALU.mult, ALU.mult), deps=(t_qs2,)))
                    d_toks.append(b.ve(lambda v, c2=c2, D2=D2, off=off:
                                       v.scalar_tensor_tensor(
                        D2.ap().rearrange("p (a x i) -> p a x i", a=16, x=2)
                        [:, :, 1, :],
                        cT[c2].ap().rearrange("p (a i) -> p a i", a=16), 1.0,
                        qs_rep[:, off + DH + 32:off + 2 * DH].unsqueeze(1)
                        .to_broadcast([128, 16, 32]),
                        ALU.mult, ALU.mult), deps=(t_qs2,)))

                qkv_ps = [b.ps("fl_qkvps", [128, 3 * INNER]) for _ in range(2)]
                qch = b.sb("fl_qch", [128, 2 * INNER], F32)
                kch = b.sb("fl_kch", [128, 2 * INNER], F32)
                sqs = b.sb("fl_sq", [128, INNER], F32)
                n2 = b.sb("fl_n2", [128, HEADS], F32)
                rr = b.sb("fl_rr", [128, HEADS], F32)
                zz = b.sb("fl_zz", [128, INNER], F32)
                t1b = b.sb("fl_t1", [128, INNER], F32)
                t2b = b.sb("fl_t2", [128, INNER], F32)
                qp = b.sb("fl_qp", [128, INNER], BF16)
                ev_toks = []
                rot_state = {}

                def rotary(chunk_ap, D1, D2, dst_fm, tc, deps):
                    st = rot_state
                    ta = b.se(lambda s: s.activation(sqs[:], chunk_ap, AF.Square),
                              deps=deps + (st.get("tb2"),))
                    tb2 = b.ve(lambda v: v.tensor_reduce(
                        n2[:], sqs.ap().rearrange("p (h i) -> p h i", h=HEADS),
                        AX.X, ALU.add), deps=(ta, st.get("tc3")))
                    tc1 = b.se(lambda s: s.activation(n2[:], n2[:], AF.Ln,
                                                      bias=cT["eps30"][:]),
                               deps=(tb2,) + tuple(init))
                    tc2 = b.se(lambda s: s.activation(n2[:], n2[:], AF.Exp,
                                                      scale=-0.5), deps=(tc1,))
                    tc3 = b.ve(lambda v: v.tensor_scalar(rr[:], n2[:], 1.0e12,
                                                         None, ALU.min),
                               deps=(tc2, st.get("tz")))
                    tz = b.ve(lambda v: v.tensor_tensor(
                        zz.ap().rearrange("p (h i) -> p h i", h=HEADS),
                        chunk_ap.rearrange("p (h i) -> p h i", h=HEADS),
                        rr.ap().unsqueeze(2).to_broadcast([128, HEADS, DH]),
                        ALU.mult), deps=(tc3, st.get("tC")))
                    tA = b.ve(lambda v: v.tensor_tensor(
                        t1b.ap().rearrange("p (h i) -> p h i", h=HEADS),
                        zz.ap().rearrange("p (h i) -> p h i", h=HEADS),
                        D1[:, tc * DH:(tc + 1) * DH].unsqueeze(1)
                        .to_broadcast([128, HEADS, DH]),
                        ALU.mult), deps=(tz, st.get("tC")) + tuple(d_toks))
                    tB0 = b.ve(lambda v: v.tensor_tensor(
                        t2b.ap().rearrange("p (h x i) -> p h x i", h=HEADS, x=2)
                        [:, :, 0, :],
                        zz.ap().rearrange("p (h x i) -> p h x i", h=HEADS, x=2)
                        [:, :, 1, :],
                        D2.ap().rearrange("p (a x i) -> p a x i", a=16, x=2)
                        [:, tc, 0, :].unsqueeze(1).to_broadcast([128, HEADS, 32]),
                        ALU.mult), deps=(tz, st.get("tC")))
                    tB1 = b.ve(lambda v: v.tensor_tensor(
                        t2b.ap().rearrange("p (h x i) -> p h x i", h=HEADS, x=2)
                        [:, :, 1, :],
                        zz.ap().rearrange("p (h x i) -> p h x i", h=HEADS, x=2)
                        [:, :, 0, :],
                        D2.ap().rearrange("p (a x i) -> p a x i", a=16, x=2)
                        [:, tc, 1, :].unsqueeze(1).to_broadcast([128, HEADS, 32]),
                        ALU.mult), deps=(tz,))
                    tC = b.ve(lambda v: v.tensor_tensor(qp[:], t1b[:], t2b[:],
                                                        ALU.add),
                              deps=(tA, tB0, tB1, st.get("ttr")))
                    ttr = b.dq(dst_fm.ap().rearrange("p (c j) -> p c j", j=T)
                               [:, :, tc * 128:(tc + 1) * 128], qp[:],
                               deps=(tC, st.get("ttr")), transpose=True)
                    st.update(tb2=tb2, tc3=tc3, tz=tz, tA=tA, tC=tC, ttr=ttr)
                    return ttr

                for tc in range(16):
                    pst = qkv_ps[tc % 2]
                    qslot = tc % 2
                    mt = None
                    for c in range(NCH):
                        for sp in range(3):
                            mt = b.mm(pst[:, sp * 512:(sp + 1) * 512],
                                      hbf[:, c * T + tc * 128:
                                          c * T + (tc + 1) * 128],
                                      wqkv[:, c * 3 * INNER + sp * 512:
                                           c * 3 * INNER + (sp + 1) * 512],
                                      start=(c == 0), stop=(c == NCH - 1),
                                      deps=(w_toks[c], h_toks[c],
                                            ev_toks[-3] if len(ev_toks) >= 3
                                            else None))
                    e_q = b.se(lambda s, pst=pst, qslot=qslot: s.activation(
                        qch[:, qslot * INNER:(qslot + 1) * INNER],
                        pst[:, 0:INNER], AF.Identity),
                        deps=(mt, rot_state.get("tz")))
                    e_k = b.se(lambda s, pst=pst, qslot=qslot: s.activation(
                        kch[:, qslot * INNER:(qslot + 1) * INNER],
                        pst[:, INNER:2 * INNER], AF.Identity),
                        deps=(mt, rot_state.get("tz")))
                    e_v = b.se(lambda s, pst=pst, tc=tc: s.activation(
                        v_tm[:, tc * INNER:(tc + 1) * INNER],
                        pst[:, 2 * INNER:3 * INNER], AF.Identity), deps=(mt,))
                    ev_toks += [e_q, e_k, e_v]
                    rotary(qch[:, qslot * INNER:(qslot + 1) * INNER],
                           Dt["q"][0], Dt["q"][1], qf_bf, tc, (e_q,))
                    rotary(kch[:, qslot * INNER:(qslot + 1) * INNER],
                           Dt["k"][0], Dt["k"][1], kf_bf, tc, (e_k,))

            with b.phase():
                Ptile = [b.sb("fl_P", [128, 8 * 512], BF16) for _ in range(2)]
                pz_init = [b.ve(lambda v, pp=pp: v.memset(pp[:], 0.0))
                           for pp in Ptile]
                ecol = b.sb("fl_ecol", [128, 64], BF16)
                ec0 = b.ve(lambda v: v.memset(ecol[:], 0.0))
                ec_toks = [b.ve(lambda v, h=h: v.memset(
                    ecol[:, 8 * h + h:8 * h + h + 1], 1.0), deps=(ec0,))
                    for h in range(HEADS)]
                att_es = contextlib.ExitStack()
                ps_sim = [att_es.enter_context(b.nc.psum_tensor(
                    b.name("fl_sim"), [128, 512], F32)) for _ in range(4)]
                sim_tok = [None] * 4
                sim_rr = [0]
                ps_z8 = att_es.enter_context(b.nc.psum_tensor(
                    b.name("fl_z8"), [8, 512], F32))
                ps_o = [att_es.enter_context(b.nc.psum_tensor(
                    b.name("fl_o"), [128, 512], F32)) for _ in range(2)]
                zrow = b.sb("fl_zrow", [HEADS, T], F32)
                rinv = b.sb("fl_rinv", [HEADS, T], F32)
                o_fm = b.sb("fl_ofm", [128, 4 * T], F32)
                obf = b.sb("fl_obf", [128, 4 * T], BF16)
                wo_sb = b.sb("fl_wo", [128, 4 * D], BF16)
                wo_toks = [b.dg(wo_sb[:, ic * D:(ic + 1) * D],
                                dr["f_wo"][lidx, ic * 128:(ic + 1) * 128, :])
                           for ic in range(4)]
                prev_o = {(0, 0): None, (64, 0): None, (0, 1): None, (64, 1): None}
                pt_last = pz_init[:]
                att_toks = []
                zev_toks = []

                def tile_cls(kt, w):
                    out = []
                    for qsub in range(4):
                        if w == 0:
                            if kt < 4:
                                out.append((qsub, "skip"))
                                continue
                            rel = 128 * (kt - 4)
                        else:
                            rel = 128 * kt - 512
                        delta = 128 * qsub - rel
                        if delta in (128, 256, 384):
                            out.append((qsub, "full"))
                        elif delta == 0:
                            out.append((qsub, "low"))
                        elif delta == 512:
                            out.append((qsub, "up"))
                        else:
                            out.append((qsub, "skip"))
                    return out

                for w in range(4):
                    zt = None
                    for h in range(HEADS):
                        hb, ho = h // 2, 64 * (h % 2)
                        q0 = w * 512
                        pslot = (w * HEADS + h) % 2
                        PtS = Ptile[pslot]
                        ktile_valid = []
                        ptoks = []
                        for kt in range(8):
                            cls = tile_cls(kt, w)
                            spans = [qs for qs, kd in cls if kd != "skip"]
                            if not spans:
                                continue
                            if w == 0:
                                ktc = kt - 4
                            else:
                                ktc = 4 * (w - 1) + kt
                            ktile_valid.append((kt, ktc))
                            sl = sim_rr[0] % 4
                            sim_rr[0] += 1
                            s0, s1 = min(spans) * 128, (max(spans) + 1) * 128
                            mt = b.mm(ps_sim[sl][:, s0:s1],
                                      kf_bf[:, hb * T + ktc * 128:
                                            hb * T + (ktc + 1) * 128]
                                      [ho:ho + 64, :],
                                      qf_bf[:, hb * T + q0 + s0:
                                            hb * T + q0 + s1][ho:ho + 64, :],
                                      start=True, stop=True,
                                      deps=(sim_tok[sl],))
                            mtoks = [mt]
                            for qsub, kd in cls:
                                if kd in ("low", "up"):
                                    mtbl = c_mlow if kd == "low" else c_mup
                                    mtoks.append(b.ve(
                                        lambda v, sl=sl, qsub=qsub, mtbl=mtbl:
                                        v.tensor_tensor(
                                            ps_sim[sl][:, qsub * 128:
                                                       (qsub + 1) * 128],
                                            ps_sim[sl][:, qsub * 128:
                                                       (qsub + 1) * 128],
                                            mtbl[:], ALU.add),
                                        deps=(mt,) + tuple(init)))
                            pe = b.se(lambda s, kt=kt, sl=sl, s0=s0, s1=s1,
                                      PtS=PtS: s.activation(
                                          PtS[:, kt * 512 + s0:kt * 512 + s1],
                                          ps_sim[sl][:, s0:s1], AF.Exp),
                                      deps=tuple(mtoks) + (pt_last[pslot],))
                            sim_tok[sl] = pe
                            ptoks.append(pe)
                        zt = None
                        for i, (kt, ktc) in enumerate(ktile_valid):
                            zt = b.mm(ps_z8[:], ecol[:, 8 * h:8 * h + 8],
                                      PtS[:, kt * 512:(kt + 1) * 512],
                                      start=(h == 0 and i == 0),
                                      stop=(h == HEADS - 1 and
                                            i == len(ktile_valid) - 1),
                                      deps=tuple(ptoks) + tuple(ec_toks) +
                                      (zev_toks[-1] if zev_toks else None,))
                        oslot = (w * HEADS + h) % 2
                        avt = None
                        for i, (kt, ktc) in enumerate(ktile_valid):
                            avt = b.mm(ps_o[oslot][ho:ho + 64, :],
                                       v_tm.ap().rearrange(
                                           "p (tc e) -> p tc e", tc=16)
                                       [:, ktc, h * DH:(h + 1) * DH],
                                       PtS[:, kt * 512:(kt + 1) * 512],
                                       start=(i == 0),
                                       stop=(i == len(ktile_valid) - 1),
                                       deps=tuple(ptoks) +
                                       (prev_o[(ho, oslot)],),
                                       tile_position=(0, ho))
                        oev = b.se(lambda s, hb=hb, ho=ho, w=w, oslot=oslot:
                                   s.activation(
                                       o_fm[:, hb * T + w * 512:
                                            hb * T + (w + 1) * 512]
                                       [ho:ho + 64, :],
                                       ps_o[oslot][ho:ho + 64, :], AF.Identity),
                                   deps=(avt,))
                        prev_o[(ho, oslot)] = oev
                        pt_last[pslot] = avt
                        att_toks.append(oev)
                    zev = b.se(lambda s, w=w: s.activation(
                        zrow[:, w * 512:(w + 1) * 512], ps_z8[:], AF.Identity),
                        deps=(zt,))
                    zev_toks.append(zev)
                    t_ri = b.ve(lambda v, w=w: v.reciprocal(
                        rinv[:, w * 512:(w + 1) * 512],
                        zrow[:, w * 512:(w + 1) * 512]),
                        deps=(zev,) + tuple(att_toks[-HEADS:]))
                    for cb in range(4):
                        sl = sim_rr[0] % 4
                        sim_rr[0] += 1
                        tmm = b.mm(ps_sim[sl][:],
                                   c_e8[:, cb * 128:(cb + 1) * 128],
                                   rinv[:, w * 512:(w + 1) * 512],
                                   start=True, stop=True,
                                   deps=(t_ri, sim_tok[sl]))
                        sim_tok[sl] = b.ve(
                            lambda v, cb=cb, w=w, sl=sl: v.tensor_tensor(
                                obf[:, cb * T + w * 512:
                                    cb * T + (w + 1) * 512],
                                o_fm[:, cb * T + w * 512:
                                     cb * T + (w + 1) * 512],
                                ps_sim[sl][:], ALU.mult), deps=(tmm,))
                b.barrier()
                att_es.close()
                ps_wo = [b.ps("fl_wops", [128, 512]) for _ in range(2)]
                ftile = b.sb("fl_ft", [128, 2 * 512], F32)
                fnew = b.sb("fl_fn", [128, 2 * 512], F32)
                wo_out = []
                for e in range(NCH):
                    for hh in range(NT4):
                        pst = ps_wo[(e * NT4 + hh) % 2]
                        tsl = (e * NT4 + hh) % 2
                        ft = ftile[:, tsl * 512:(tsl + 1) * 512]
                        fn = fnew[:, tsl * 512:(tsl + 1) * 512]
                        mt = None
                        for ic in range(4):
                            mt = b.mm(pst[:],
                                      wo_sb[:, ic * D + e * 128:
                                            ic * D + (e + 1) * 128],
                                      obf[:, ic * T + hh * 512:
                                          ic * T + (hh + 1) * 512],
                                      start=(ic == 0), stop=(ic == 3),
                                      deps=(wo_toks[ic],
                                            wo_out[-2] if len(wo_out) >= 2
                                            else None))
                        lf = b.dq(ft,
                                  f_src[e * 128:(e + 1) * 128,
                                        hh * 512:(hh + 1) * 512],
                                  deps=(wo_out[-2] if len(wo_out) >= 2
                                        else None,))
                        ta = b.ve(lambda v, pst=pst, ft=ft, fn=fn:
                                  v.tensor_tensor(fn, pst[:], ft, ALU.add),
                            deps=(mt, lf,
                                  wo_out[-2] if len(wo_out) >= 2 else None))
                        wo_out.append(b.dq(
                            f_mid[e * 128:(e + 1) * 128,
                                  hh * 512:(hh + 1) * 512],
                            fn, deps=(ta,)))

        if stop_after_c:
            return
        with b.phase():
            gsb = b.sb("fl_g", [128, NCH], F32)
            bsb = b.sb("fl_b", [128, NCH], F32)
            tg = b.dq(gsb.ap(), dr["f_lng"][lidx].rearrange("(a p) -> p a", p=128), slow=True)
            tb = b.dq(bsb.ap(), dr["f_lnb"][lidx].rearrange("(a p) -> p a", p=128), slow=True)
            hbf2 = b.sb("fl_h2", [128, NCH * T], BF16)
            with b.phase():
                b.barrier()
                ln_to_sbuf_bf(f_mid, hbf2,
                              gain_bias=None if DBG.get("nogain") else (gsb, bsb),
                              shift=not DBG.get("noshift"))
            if DBG.get("ffn_cut") == "l0ln2" and lidx == 0:
                with b.phase():
                    cp = b.sb("dbg_cp", [128, T], F32)
                    prev = None
                    for c in range(NCH):
                        t1 = b.ve(lambda v, c=c: v.tensor_copy(
                            cp[:], hbf2[:, c * T:(c + 1) * T]), deps=(prev,))
                        prev = b.dq(x_out[c * 128:(c + 1) * 128, :], cp[:],
                                    deps=(t1,))
                return
            w1_sb = b.sb("fl_w1", [128, 2 * NCH * 128], BF16)
            b1_sb = b.sb("fl_b1", [128, 2 * NH1], F32)
            tb1 = []
            for sh in range(2):
                for c in range(NH1):
                    wdt = 42 if c == NH1 - 1 else 128
                    tb1.append(b.dq(
                        b1_sb[0:wdt, sh * NH1 + c:sh * NH1 + c + 1],
                        dr["f_b1"][lidx, sh * HID + c * 128:
                                   sh * HID + c * 128 + wdt].unsqueeze(1)))
            w1_es = contextlib.ExitStack()
            ps_h = [w1_es.enter_context(
                b.nc.psum_tensor(b.name("fl_hps"), [128, 512], F32))
                for _ in range(8)]
            hps_tok = [None] * 8
            hps_rr = [0]
            ag = b.sb("fl_ag", [128, NH1 * T], BF16)
            atmp = b.sb("fl_at", [128, T], F32)
            gtmp = b.sb("fl_gt", [128, T], F32)

            def w1_chunk(sh, c, slot, deps):
                wdt = 42 if c == NH1 - 1 else 128
                col0 = sh * HID + c * 128
                return b.dg(
                    w1_sb[:, slot * NCH * 128:(slot + 1) * NCH * 128]
                    .rearrange("p (a k) -> p a k", a=NCH)[:, :, 0:wdt],
                    dr["f_w1"].rearrange("l (a p) hh -> l p a hh", p=128)
                    [lidx, :, :, col0:col0 + wdt], deps=deps)

            geg = []
            prev_a = [None]
            prev_g = [None]
            order = [(sh, c) for c in range(NH1) for sh in (0, 1)]
            wtoks = [w1_chunk(0, 0, 0, ()), None]
            for idx, (sh, c) in enumerate(order):
                slot = idx % 2
                wdt = 42 if c == NH1 - 1 else 128
                pair_ps = []
                for hh in range(NT4):
                    sl = hps_rr[0] % 8
                    hps_rr[0] += 1
                    mt = None
                    for dc in range(NCH):
                        mt = b.mm(ps_h[sl][0:wdt, :],
                                  w1_sb[:, slot * NCH * 128:
                                        (slot + 1) * NCH * 128]
                                  .rearrange("p (a k) -> p a k", a=NCH)
                                  [:, dc, 0:wdt],
                                  hbf2[:, dc * T + hh * 512:
                                       dc * T + (hh + 1) * 512],
                                  start=(dc == 0), stop=(dc == NCH - 1),
                                  deps=(wtoks[slot], hps_tok[sl]))
                    pair_ps.append((sl, mt, hh))
                if idx + 1 < len(order):
                    ns, ncol = order[idx + 1]
                    wtoks[(idx + 1) % 2] = w1_chunk(
                        ns, ncol, (idx + 1) % 2,
                        (geg[-1] if geg else None,))
                if sh == 0:
                    war = geg[-1] if geg else None
                    for sl, mt, hh in pair_ps:
                        ta = b.se(lambda s, sl=sl, wdt=wdt, c=c, hh=hh:
                                  s.activation(
                                      atmp[0:wdt, hh * 512:(hh + 1) * 512],
                                      ps_h[sl][0:wdt, :], AF.Identity,
                                      bias=b1_sb[0:wdt, c:c + 1]),
                                  deps=(mt, war) + tuple(tb1))
                        hps_tok[sl] = ta
                        geg.append(ta)
                else:
                    for sl, mt, hh in pair_ps:
                        tg2 = b.se(lambda s, sl=sl, wdt=wdt, c=c, hh=hh:
                                   s.activation(
                                       gtmp[0:wdt, hh * 512:(hh + 1) * 512],
                                       ps_h[sl][0:wdt, :], AF.Gelu,
                                       bias=b1_sb[0:wdt, NH1 + c:NH1 + c + 1]),
                                   deps=(mt, prev_g[0]) + tuple(tb1))
                        hps_tok[sl] = tg2
                        tm = b.ve(lambda v, wdt=wdt, c=c, hh=hh:
                                  v.tensor_tensor(
                                      ag[0:wdt, c * T + hh * 512:
                                         c * T + (hh + 1) * 512],
                                      atmp[0:wdt, hh * 512:(hh + 1) * 512],
                                      gtmp[0:wdt, hh * 512:(hh + 1) * 512],
                                      ALU.mult), deps=(tg2,))
                        geg.append(tm)
                    prev_g[0] = geg[-1]
            b.barrier()
            w1_es.close()
            if DBG.get("tap_ag") and lidx == 0:
                prevt = None
                for c in range(NH1):
                    wdt = 42 if c == NH1 - 1 else 128
                    prevt = b.dq(tap_ag[c * 128:c * 128 + wdt, :],
                                 ag[0:wdt, c * T:(c + 1) * T], deps=(prevt,))
                b.barrier()

            w2_sb = b.sb("fl_w2", [128, 2 * NH1 * 128], BF16)
            b2_sb = b.sb("fl_b2", [128, NCH], F32)
            tb2_ = b.dq(b2_sb.ap(), dr["f_b2"][lidx].rearrange("(a p) -> p a",
                                                              p=128), slow=True)
            ps_w2 = [b.ps("fl_w2ps", [128, 512]) for _ in range(2)]
            f2 = b.sb("fl_f2", [128, 2 * 512], F32)
            x2t = b.sb("fl_x2", [128, 2 * 512], F32)
            res2 = b.sb("fl_r2", [128, 2 * 512], F32)
            w2_out = []

            def w2_chunk(e, slot, deps):
                return b.dg(
                    w2_sb[:, slot * NH1 * 128:(slot + 1) * NH1 * 128]
                    .rearrange("p (a k) -> p a k", a=NH1),
                    dr["f_w2p"].rearrange("l (a p) e -> l p a e", p=128)
                    [lidx, :, :, e * 128:(e + 1) * 128], deps=deps)

            w2toks = [w2_chunk(0, 0, ()), None]
            w2_last_mm = [None, None]
            x_src2, x_dst2 = dst_pair if dst_pair[0] is not None else (None, None)
            f_out = dst_pair[1]
            for e in range(NCH):
                slot = e % 2
                if e + 1 < NCH:
                    w2toks[(e + 1) % 2] = w2_chunk(
                        e + 1, (e + 1) % 2, (w2_last_mm[(e + 1) % 2],))
                for hh in range(NT4):
                    pst = ps_w2[hh % 2]
                    mt = None
                    for c in range(NH1):
                        wdt = 42 if c == NH1 - 1 else 128
                        mt = b.mm(pst[:],
                                  w2_sb[:, slot * NH1 * 128:
                                        (slot + 1) * NH1 * 128]
                                  .rearrange("p (a k) -> p a k", a=NH1)
                                  [0:wdt, c, :],
                                  ag[0:wdt, c * T + hh * 512:
                                     c * T + (hh + 1) * 512],
                                  start=(c == 0), stop=(c == NH1 - 1),
                                  deps=(w2toks[slot],
                                        w2_out[-2] if len(w2_out) >= 2
                                        else None))
                    w2_last_mm[slot] = mt
                    tsl = hh % 2
                    f2t = f2[:, tsl * 512:(tsl + 1) * 512]
                    x2s = x2t[:, tsl * 512:(tsl + 1) * 512]
                    r2t = res2[:, tsl * 512:(tsl + 1) * 512]
                    lf = b.dq(f2t, f_mid[e * 128:(e + 1) * 128,
                                         hh * 512:(hh + 1) * 512],
                              deps=(w2_out[-2] if len(w2_out) >= 2 else None,))
                    t1 = b.ve(lambda v, pst=pst, e=e, f2t=f2t, r2t=r2t:
                              v.scalar_tensor_tensor(
                                  r2t, pst[:], b2_sb[:, e:e + 1], f2t,
                                  ALU.add, ALU.add),
                        deps=(mt, lf, tb2_,
                              w2_out[-2] if len(w2_out) >= 2 else None))
                    if x_src2 is not None:
                        lx = b.dq(x2s, x_src2[e * 128:(e + 1) * 128,
                                              hh * 512:(hh + 1) * 512],
                                  deps=(w2_out[-2] if len(w2_out) >= 2
                                        else None,))
                        t1 = b.ve(lambda v, r2t=r2t, x2s=x2s: v.tensor_tensor(
                            r2t, r2t, x2s, ALU.add), deps=(t1, lx))
                        dst = x_dst2
                    else:
                        dst = f_out
                    w2_out.append(b.dq(
                        dst[e * 128:(e + 1) * 128, hh * 512:(hh + 1) * 512],
                        r2t, deps=(t1,)))

    def ffn_call(x_src, x_dst, cut=None):
        with b.phase():
            ln_to_dram_f32(x_src, f_A)
        if cut == "ln":
            return f_A
        ffn_inner(0, f_A, f_B, (None, f_A), stop_after_c=(cut == "l0mid"))
        if cut == "l0mid":
            return f_B
        if cut == "l0ln2":
            return None
        if cut == "l0out":
            return f_A
        ffn_inner(1, f_A, f_B, (x_src, x_dst))
        return None

    def copy_out(src):
        with b.phase():
            tbuf = b.sb("cp", [128, T], F32)
            prev = None
            for c in range(NCH):
                l = b.dq(tbuf[:], src[c * 128:(c + 1) * 128, :], deps=(prev,))
                prev = b.dq(x_out[c * 128:(c + 1) * 128, :], tbuf[:], deps=(l,))

    attn_causal("a", x_in, x_A)
    if stages == 1:
        copy_out(x_A)
    if stages >= 2:
        cut = DBG.get("ffn_cut") if stages == 2 else None
        r = ffn_call(x_A, x_B if stages > 2 else x_out, cut=cut)
        if r is not None:
            copy_out(r)
    if stages >= 3:
        attn_causal("b", x_B, x_A if stages > 3 else x_out)
    if stages >= 4:
        ffn_call(x_A, x_out)
    b.barrier()

    es.close()
    return nc


def _host_prepare(inputs):
    X = np.asarray(inputs["L0_attn_a_X"], dtype=np.float32)
    Xfm = np.ascontiguousarray(X.transpose(0, 2, 1))

    def t(a):
        return np.ascontiguousarray(np.asarray(a, dtype=np.float32).T)

    qs = np.asarray(inputs["L0_ffn_q_scale"], dtype=np.float32)
    ks = np.asarray(inputs["L0_ffn_k_scale"], dtype=np.float32)
    rot = (np.arange(DH) + 32) % DH
    w2 = np.asarray(inputs["L0_ffn_w2"], dtype=np.float32)
    w2p = np.zeros((2, HIDPAD, D), dtype=np.float32)
    w2p[:, :HID, :] = w2

    qi = np.arange(128)
    m_diag = np.where(np.arange(128)[None, :] <= qi[:, None], 0.0,
                      NEG).astype(np.float32)
    kk = np.arange(128)[:, None]
    qj = np.arange(128)[None, :]
    m_low = np.where(qj >= kk, 0.0, NEG).astype(np.float32)
    m_up = np.where(qj <= kk, 0.0, NEG).astype(np.float32)
    e8 = np.zeros((8, 512), dtype=np.float32)
    for cb in range(4):
        for p in range(128):
            e8[2 * cb + p // 64, cb * 128 + p] = 1.0

    shared = {
        "aWqT": t(inputs["L0_attn_a_Wq"]), "aWkT": t(inputs["L0_attn_a_Wk"]),
        "aWvT": t(inputs["L0_attn_a_Wv"]),
        "a_bq": np.asarray(inputs["L0_attn_a_bq"], np.float32),
        "a_bk": np.asarray(inputs["L0_attn_a_bk"], np.float32),
        "a_bv": np.asarray(inputs["L0_attn_a_bv"], np.float32),
        "bWqT": t(inputs["L0_attn_b_Wq"]), "bWkT": t(inputs["L0_attn_b_Wk"]),
        "bWvT": t(inputs["L0_attn_b_Wv"]), "bWoT": t(inputs["L0_attn_b_Wo"]),
        "b_bq": np.asarray(inputs["L0_attn_b_bq"], np.float32),
        "b_bk": np.asarray(inputs["L0_attn_b_bk"], np.float32),
        "b_bv": np.asarray(inputs["L0_attn_b_bv"], np.float32),
        "b_bo": np.asarray(inputs["L0_attn_b_bo"], np.float32),
        "f_wqkv": np.ascontiguousarray(inputs["L0_ffn_wqkv"], np.float32),
        "f_wo": np.ascontiguousarray(inputs["L0_ffn_wo"], np.float32),
        "f_w1": np.ascontiguousarray(inputs["L0_ffn_w1"], np.float32),
        "f_w2p": w2p,
        "f_qs": np.ascontiguousarray(qs), "f_ks": np.ascontiguousarray(ks),
        "f_qs_rot": np.ascontiguousarray(qs[:, rot]),
        "f_ks_rot": np.ascontiguousarray(ks[:, rot]),
        "f_lng": np.ascontiguousarray(inputs["L0_ffn_ln_g"], np.float32),
        "f_lnb": np.ascontiguousarray(inputs["L0_ffn_ln_b"], np.float32),
        "f_b1": np.ascontiguousarray(inputs["L0_ffn_b1"], np.float32),
        "f_b2": np.ascontiguousarray(inputs["L0_ffn_b2"], np.float32),
        "m_diag": m_diag, "m_low": m_low, "m_up": m_up, "e8": e8,
        "m_diag_t": np.ascontiguousarray(m_diag.T),
        "m_low_t": np.ascontiguousarray(m_low.T),
        "m_up_t": np.ascontiguousarray(m_up.T),
        "ident128": np.eye(128, dtype=np.float32),
    }
    in_maps = []
    for c in range(8):
        m = dict(shared)
        m["x_fm"] = np.ascontiguousarray(Xfm[c])
        in_maps.append(m)
    return in_maps


def kernel(**inputs):
    in_maps = _host_prepare(inputs)
    nc = build_program()
    res = run_bass_kernel_spmd(nc, in_maps, core_ids=list(range(8)))
    outs = [np.asarray(r["x_outf"]) for r in res.results]
    full = np.stack(outs, axis=0)
    return np.ascontiguousarray(full.transpose(0, 2, 1))


# version 18
# speedup vs baseline: 1.0411x; 1.0411x over previous
import math
import contextlib
import numpy as np

import concourse.bass as bass
import concourse.mybir as mybir
from concourse.bass_utils import run_bass_kernel_spmd

F32 = mybir.dt.float32
BF16 = mybir.dt.bfloat16
I32 = mybir.dt.int32
AF = mybir.ActivationFunctionType
ALU = mybir.AluOpType
AX = mybir.AxisListType

D = 1024
T = 2048
NCH = 8
NT4 = 4
NEG = -3.0e38
HEADS = 8
DH = 64
INNER = 512
HID = 2730
H2 = 5460
NH1 = 22
HIDPAD = NH1 * 128
NH_B = 16


def _hl(v):
    hi = np.float32(v)
    lo = np.float32(np.float64(v) - np.float64(hi))
    return float(hi), float(lo)


def cdiv(a, b):
    return -(-a // b)


NDMA = 8


class Builder:
    def __init__(self, nc, es):
        self.nc = nc
        self.es = es
        self.stack = [es]
        self.sems = {}
        self.cnt = {}
        self.waited = {}
        for name in ("t", "v", "s", "g"):
            self.sems[name] = es.enter_context(nc.semaphore("sem_" + name))
            self.cnt[name] = 0
        for q in ("dq", "dg"):
            for i in range(NDMA):
                nm = "%s%d" % (q, i)
                self.sems[nm] = es.enter_context(nc.semaphore("sem_" + nm))
                self.cnt[nm] = 0
        self.rr = {"dq": 0, "dg": 0}
        self.uid = 0

    def name(self, stem):
        self.uid += 1
        return "%s_%d" % (stem, self.uid)

    def sb(self, stem, shape, dtype):
        return self.stack[-1].enter_context(
            self.nc.sbuf_tensor(self.name(stem), shape, dtype))

    def ps(self, stem, shape):
        return self.stack[-1].enter_context(
            self.nc.psum_tensor(self.name(stem), shape, F32))

    @contextlib.contextmanager
    def phase(self):
        ps = contextlib.ExitStack()
        self.stack.append(ps)
        try:
            yield
        finally:
            self.barrier()
            self.stack.pop()
            ps.close()

    def _inc(self, instr, sem, n):
        self.cnt[sem] += n
        instr.then_inc(self.sems[sem], n)
        return (sem, self.cnt[sem])

    def _wait(self, eng, ename, deps):
        for tok in deps:
            if tok is None:
                continue
            sem, val = tok
            key = (ename, sem)
            if self.waited.get(key, 0) < val:
                eng.wait_ge(self.sems[sem], val)
                self.waited[key] = val

    def te(self, fn, deps=()):
        self._wait(self.nc.tensor, "t", deps)
        return self._inc(fn(self.nc.tensor), "t", 1)

    def ve(self, fn, deps=()):
        self._wait(self.nc.vector, "v", deps)
        return self._inc(fn(self.nc.vector), "v", 1)

    def se(self, fn, deps=()):
        self._wait(self.nc.scalar, "s", deps)
        return self._inc(fn(self.nc.scalar), "s", 1)

    def ge(self, fn, deps=()):
        self._wait(self.nc.gpsimd, "g", deps)
        return self._inc(fn(self.nc.gpsimd), "g", 1)

    def _dma(self, queue, eng, ename, start_fn, deps):
        i = self.rr[queue]
        self.rr[queue] = (i + 1) % NDMA
        sem = "%s%d" % (queue, i)
        if self.cnt[sem] > 0:
            self._wait(eng, ename, ((sem, self.cnt[sem]),))
        self._wait(eng, ename, deps)
        return self._inc(start_fn(), sem, 16)

    def dq(self, out, in_, deps=(), transpose=False, slow=False):
        kw = dict(transpose=transpose)
        if slow:
            kw["allow_slow_non_contiguous"] = True
        return self._dma("dq", self.nc.sync, "q",
                         lambda: self.nc.sync.dma_start(out, in_, **kw), deps)

    def dg(self, out, in_, deps=()):
        return self._dma("dg", self.nc.gpsimd, "g",
                         lambda: self.nc.gpsimd.dma_start(out, in_), deps)

    def mm(self, out, lhsT, rhs, start, stop, deps=(), tile_position=None):
        self._wait(self.nc.tensor, "t", deps)
        kw = {}
        if tile_position is not None:
            kw["tile_position"] = tile_position
        return self._inc(self.nc.tensor.matmul(out, lhsT, rhs, start=start,
                                               stop=stop, **kw), "t", 1)

    def barrier(self):
        nc = self.nc
        names = ["t", "v", "s", "g"] + ["%s%d" % (q, i) for q in ("dq", "dg")
                                        for i in range(NDMA)]
        toks = tuple((s, self.cnt[s]) for s in names)
        for eng, ename in ((nc.tensor, "t"), (nc.vector, "v"), (nc.scalar, "s"),
                           (nc.gpsimd, "g"), (nc.sync, "q")):
            self._wait(eng, ename, toks)


def build_tables(b, cT):
    tv = b.sb("tv", [128, 16], F32)
    d2 = b.sb("d2", [128, 32], F32)
    invf = b.sb("invf", [128, 32], F32)
    aux = b.sb("aux", [128, 32], F32)
    base = b.sb("base", [128, 32], F32)
    powr = b.sb("powr", [128, 16], F32)
    freqs = b.sb("freqs", [128, 512], F32)
    pw = b.sb("pw", [128, 512], F32)
    scl = b.sb("scl", [128, 512], F32)
    sclv = b.sb("sclv", [128, 512], F32)
    sinb = b.sb("sinb", [128, 512], F32)
    cosb = b.sb("cosb", [128, 512], F32)
    karg = b.sb("karg", [128, 512], F32)
    ktmp = b.sb("ktmp", [128, 512], F32)
    ktmp2 = b.sb("ktmp2", [128, 512], F32)
    ki = b.sb("ki", [128, 512], I32)
    kf = b.sb("kf", [128, 512], F32)
    sg = b.sb("sg", [128, 512], F32)

    t_tv = b.ge(lambda g: g.iota(tv[:], [[128, 16]], channel_multiplier=1,
                                 allow_small_or_imprecise_dtypes=True))
    t_d2 = b.ge(lambda g: g.iota(d2[:], [[2, 32]], channel_multiplier=0,
                                 allow_small_or_imprecise_dtypes=True))

    chi, clo = _hl(math.log(10000.0) / 64.0)
    t1 = b.ve(lambda v: v.tensor_scalar(invf[:], d2[:], -chi, None, ALU.mult),
              deps=(t_d2,))
    t2 = b.ve(lambda v: v.tensor_scalar(aux[:], d2[:], -clo, None, ALU.mult),
              deps=(t_d2,))
    t3 = b.ve(lambda v: v.tensor_tensor(invf[:], invf[:], aux[:], ALU.add),
              deps=(t1, t2))
    t4 = b.se(lambda s: s.activation(invf[:], invf[:], AF.Exp), deps=(t3,))

    t5 = b.ve(lambda v: v.tensor_tensor(
        freqs.ap().rearrange("p (a i) -> p a i", a=16),
        tv.ap().unsqueeze(2).to_broadcast([128, 16, 32]),
        invf.ap().unsqueeze(1).to_broadcast([128, 16, 32]),
        ALU.mult), deps=(t_tv, t4))

    ahi, alo = _hl(25.6)
    rhi, rlo = _hl(1.0 / 89.6)
    tb1 = b.ve(lambda v: v.tensor_scalar(base[:], d2[:], ahi, alo, ALU.add,
                                         ALU.add), deps=(t_d2,))
    tb2 = b.ve(lambda v: v.tensor_scalar(aux[:], base[:], rhi, None, ALU.mult),
               deps=(tb1,))
    tb3 = b.ve(lambda v: v.tensor_scalar(base[:], base[:], rlo, None, ALU.mult),
               deps=(tb2,))
    tb4 = b.ve(lambda v: v.tensor_tensor(base[:], base[:], aux[:], ALU.add),
               deps=(tb2, tb3))
    tb5 = b.se(lambda s: s.activation(base[:], base[:], AF.Ln), deps=(tb4,))
    tb6 = b.ve(lambda v: v.tensor_scalar(powr[:], tv[:], 1.0 / 256.0, 4.0,
                                         ALU.mult, ALU.subtract), deps=(t_tv,))
    tb7 = b.ve(lambda v: v.tensor_tensor(
        pw.ap().rearrange("p (a i) -> p a i", a=16),
        powr.ap().unsqueeze(2).to_broadcast([128, 16, 32]),
        base.ap().unsqueeze(1).to_broadcast([128, 16, 32]),
        ALU.mult), deps=(tb5, tb6))
    t_scl = b.se(lambda s: s.activation(scl[:], pw[:], AF.Exp), deps=(tb7,))
    t_sclv = b.se(lambda s: s.activation(sclv[:], pw[:], AF.Exp, scale=-1.0),
                  deps=(tb7,))

    p1, p2 = _hl(2.0 * math.pi)
    p3 = float(np.float32(np.float64(2.0 * math.pi) - np.float64(p1)
                          - np.float64(p2)))
    pih, pil = _hl(math.pi)
    inv2pi = float(np.float32(1.0 / (2.0 * math.pi)))
    THR = 3.1415925

    def sincos(shift_half_pi, out_sb, deps):
        if shift_half_pi:
            hh, hl2 = _hl(math.pi / 2.0)
            d1 = b.ve(lambda v: v.tensor_scalar(karg[:], freqs[:], hh, hl2,
                                                ALU.add, ALU.add), deps=deps)
        else:
            d1 = b.ve(lambda v: v.tensor_copy(karg[:], freqs[:]), deps=deps)
        d2_ = b.ve(lambda v: v.tensor_scalar(ktmp[:], karg[:], inv2pi, None,
                                             ALU.mult), deps=(d1,))
        d3 = b.ve(lambda v: v.tensor_copy(ki[:], ktmp[:]), deps=(d2_,))
        d4 = b.ve(lambda v: v.tensor_copy(kf[:], ki[:]), deps=(d3,))
        d5 = b.ve(lambda v: v.scalar_tensor_tensor(karg[:], kf[:], -p1, karg[:],
                                                   ALU.mult, ALU.add), deps=(d4,))
        d6 = b.ve(lambda v: v.scalar_tensor_tensor(karg[:], kf[:], -p2, karg[:],
                                                   ALU.mult, ALU.add), deps=(d5,))
        d7 = b.ve(lambda v: v.scalar_tensor_tensor(karg[:], kf[:], -p3, karg[:],
                                                   ALU.mult, ALU.add), deps=(d6,))
        d8 = b.ve(lambda v: v.tensor_scalar(ktmp[:], karg[:], THR, None,
                                            ALU.is_gt), deps=(d7,))
        d9 = b.ve(lambda v: v.tensor_scalar(ktmp2[:], karg[:], -THR, None,
                                            ALU.is_lt), deps=(d7,))
        d10 = b.ve(lambda v: v.tensor_tensor(ktmp[:], ktmp[:], ktmp2[:],
                                             ALU.subtract), deps=(d8, d9))
        d11 = b.ve(lambda v: v.scalar_tensor_tensor(karg[:], ktmp[:], -pih,
                                                    karg[:], ALU.mult, ALU.add),
                   deps=(d10,))
        d12 = b.ve(lambda v: v.scalar_tensor_tensor(karg[:], ktmp[:], -pil,
                                                    karg[:], ALU.mult, ALU.add),
                   deps=(d11,))
        d13 = b.ve(lambda v: v.tensor_tensor(ktmp2[:], ktmp[:], ktmp[:],
                                             ALU.mult), deps=(d12,))
        d14 = b.ve(lambda v: v.tensor_scalar(ktmp2[:], ktmp2[:], -2.0, 1.0,
                                             ALU.mult, ALU.add), deps=(d13,))
        d15 = b.se(lambda s: s.activation(sg[:], karg[:], AF.Sin), deps=(d12,))
        return b.ve(lambda v: v.tensor_tensor(out_sb[:], sg[:], ktmp2[:],
                                              ALU.mult), deps=(d14, d15))

    t_sin = sincos(False, sinb, (t5,))
    t_cos = sincos(True, cosb, (t5, t_sin))

    toks = []
    for nm, trig, trig_tok, sc, sc_tok, mulc in (
            ("c1q", cosb, t_cos, scl, t_scl, 8.0),
            ("c2q", sinb, t_sin, scl, t_scl, 8.0),
            ("c1k", cosb, t_cos, sclv, t_sclv, 1.0),
            ("c2k", sinb, t_sin, sclv, t_sclv, 1.0)):
        dst = cT[nm]
        toks.append(b.ve(lambda v, a=trig, c=sc, o=dst, m=mulc:
                         v.scalar_tensor_tensor(o[:], a[:], m, c[:],
                                                ALU.mult, ALU.mult),
                         deps=(trig_tok, sc_tok, t_cos)))
    return toks


DBG = {}


def build_program(stages=4):
    nc = bass.Bass("TRN2", target_bir_lowering=False,
                   detect_race_conditions=bool(DBG.get("race")))
    es = contextlib.ExitStack()
    b = Builder(nc, es)

    dr = {}

    def din(name, shape, dtype=F32):
        dr[name] = nc.dram_tensor(name, shape, dtype, kind="ExternalInput").ap()
        return dr[name]

    x_in = din("x_fm", [D, T])
    for nm in ("aWqT", "aWkT", "aWvT", "bWqT", "bWkT", "bWvT", "bWoT"):
        din(nm, [D, D])
    for nm in ("a_bq", "a_bk", "a_bv", "b_bq", "b_bk", "b_bv", "b_bo"):
        din(nm, [D])
    din("f_wqkv", [2, D, 3 * INNER])
    din("f_wo", [2, INNER, D])
    din("f_w1", [2, D, H2])
    din("f_w2p", [2, HIDPAD, D])
    din("f_qs", [2, DH])
    din("f_ks", [2, DH])
    din("f_qs_rot", [2, DH])
    din("f_ks_rot", [2, DH])
    din("f_lng", [2, D])
    din("f_lnb", [2, D])
    din("f_b1", [2, H2])
    din("f_b2", [2, D])
    din("m_diag", [128, 128])
    din("m_diag_t", [128, 128])
    din("m_low_t", [128, 128])
    din("m_up_t", [128, 128])
    din("ident128", [128, 128])
    din("m_low", [128, 128])
    din("m_up", [128, 128])
    din("e8", [8, 512])

    x_out = nc.dram_tensor("x_outf", [D, T], F32, kind="ExternalOutput").ap()
    x_A = nc.dram_tensor("x_A", [D, T], F32).ap()
    x_B = nc.dram_tensor("x_B", [D, T], F32).ap()
    f_A = nc.dram_tensor("f_A", [D, T], F32).ap()
    f_B = nc.dram_tensor("f_B", [D, T], F32).ap()
    ao_d = nc.dram_tensor("ao_d", [D, T], BF16).ap()
    tap_ag = nc.dram_tensor("tap_ag", [HIDPAD, T], BF16).ap()

    cT = {}
    cT["ones_bf"] = b.sb("ones_bf", [128, 1], BF16)
    cT["ones_f"] = b.sb("ones_f", [1, 128], F32)
    c_mdiag = b.sb("c_mdiag", [128, 128], F32)
    c_mdT_bf = b.sb("c_mdT_bf", [128, 128], BF16)
    c_mlT_bf = b.sb("c_mlT_bf", [128, 128], BF16)
    c_muT_bf = b.sb("c_muT_bf", [128, 128], BF16)
    c_id_bf = b.sb("c_id_bf", [128, 128], BF16)
    c_mlow = b.sb("c_mlow", [128, 128], F32)
    c_mup = b.sb("c_mup", [128, 128], F32)
    c_e8 = b.sb("c_e8", [8, 512], F32)

    cT["eps5"] = b.sb("c_eps5", [128, 1], F32)
    cT["eps30"] = b.sb("c_eps30", [128, 1], F32)
    init = []
    init.append(b.ve(lambda v: v.memset(cT["eps5"][:], 1.0e-5)))
    init.append(b.ve(lambda v: v.memset(cT["eps30"][:], 1.0e-30)))
    init.append(b.ve(lambda v: v.memset(cT["ones_bf"][:], 1.0)))
    init.append(b.ve(lambda v: v.memset(cT["ones_f"][:], 1.0)))
    init.append(b.dq(c_mdiag[:], dr["m_diag"]))
    init.append(b.dg(c_mdT_bf[:], dr["m_diag_t"]))
    init.append(b.dg(c_mlT_bf[:], dr["m_low_t"]))
    init.append(b.dg(c_muT_bf[:], dr["m_up_t"]))
    init.append(b.dg(c_id_bf[:], dr["ident128"]))
    init.append(b.dq(c_mlow[:], dr["m_low"]))
    init.append(b.dq(c_mup[:], dr["m_up"]))
    init.append(b.dq(c_e8[:], dr["e8"]))
    for nm in ("c1q", "c2q", "c1k", "c2k"):
        cT[nm] = b.sb(nm, [128, 512], F32)
    with b.phase():
        build_tables(b, cT)

    def emit_ln_core(x_loader, out_writer, gain_bias=None):
        xf = b.sb("ln_x", [128, 2 * T], F32)
        xb = b.sb("ln_xb", [128, 2 * T], BF16)
        xq = b.sb("ln_xq", [128, T], BF16)
        srow = b.sb("ln_srow", [1, 2 * T], F32)
        vrow = b.sb("ln_vrow", [1, T], F32)
        nrow = srow[:, 0:T]
        ps_es = contextlib.ExitStack()
        ps_s = ps_es.enter_context(
            b.nc.psum_tensor(b.name("ln_ps"), [1, 2 * T], F32))
        mm_tok = None
        prev_use = [None, None]
        prev_mm = [None, None]
        for c in range(NCH):
            sl = c % 2
            o = sl * T
            ld = x_loader(c, xf[:, o:o + T], prev_use[sl])
            tc = b.se(lambda s, o=o: s.activation(xb[:, o:o + T], xf[:, o:o + T],
                                                  AF.Identity),
                      deps=(ld, prev_mm[sl]))
            tq = b.se(lambda s, o=o: s.activation(xq[:], xf[:, o:o + T],
                                                  AF.Square),
                      deps=(ld, mm_tok))
            prev_use[sl] = tq
            for h in range(NT4):
                mm_tok = b.mm(ps_s[:, h * 512:(h + 1) * 512], cT["ones_bf"][:],
                              xb[:, o + h * 512:o + (h + 1) * 512],
                              start=(c == 0), stop=(c == NCH - 1),
                              deps=(tc, mm_tok))
                mm_tok = b.mm(ps_s[:, T + h * 512:T + (h + 1) * 512],
                              cT["ones_bf"][:], xq[:, h * 512:(h + 1) * 512],
                              start=(c == 0), stop=(c == NCH - 1), deps=(tq,))
            prev_mm[sl] = mm_tok
        ev = b.ve(lambda v: v.tensor_copy(srow[:], ps_s[:]), deps=(mm_tok,))
        ps_es.close()
        t_m2 = b.ve(lambda v: v.scalar_tensor_tensor(
            vrow[:], srow[:, 0:T], 1.0 / D, srow[:, 0:T], ALU.mult, ALU.mult),
            deps=(ev,))
        t_m3 = b.ve(lambda v: v.tensor_scalar(vrow[:], vrow[:], 1.0 / D, None,
                                              ALU.mult), deps=(t_m2,))
        t_var = b.ve(lambda v: v.scalar_tensor_tensor(
            vrow[:], srow[:, T:2 * T], 1.0 / D, vrow[:], ALU.mult, ALU.subtract),
            deps=(t_m3,))
        t_ln = b.se(lambda s: s.activation(vrow[:], vrow[:], AF.Ln,
                                           bias=cT["eps5"][0:1, :]),
                    deps=(t_var,) + tuple(init))
        t_r = b.se(lambda s: s.activation(vrow[:], vrow[:], AF.Exp, scale=-0.5),
                   deps=(t_ln,))
        t_nm = b.ve(lambda v: v.tensor_scalar(nrow, srow[:, 0:T], -1.0 / D, None,
                                              ALU.mult), deps=(t_m2, t_var))
        ps_es2 = contextlib.ExitStack()
        ps_rep = ps_es2.enter_context(
            b.nc.psum_tensor(b.name("ln_rep"), [128, 2 * T], F32))
        reps = []
        for h in range(NT4):
            reps.append(b.mm(ps_rep[:, h * 512:(h + 1) * 512], cT["ones_f"][:],
                             vrow[:, h * 512:(h + 1) * 512], start=True,
                             stop=True, deps=(t_r,)))
            reps.append(b.mm(ps_rep[:, T + h * 512:T + (h + 1) * 512],
                             cT["ones_f"][:], nrow[0:1, h * 512:(h + 1) * 512],
                             start=True, stop=True, deps=(t_nm,)))
        utmp = b.sb("ln_ut", [128, T], F32)
        outs = []
        prev_wr = None
        prev_use2 = [mm_tok, mm_tok]
        for c in range(NCH):
            sl = c % 2
            o = sl * T
            ld = x_loader(c, xf[:, o:o + T], prev_use2[sl])
            t_u = b.ve(lambda v, o=o: v.tensor_tensor(
                utmp[:], xf[:, o:o + T], ps_rep[:, T:2 * T], ALU.add),
                deps=tuple(reps) + (ld, prev_wr))
            prev_use2[sl] = t_u
            t_u2 = b.ve(lambda v: v.tensor_tensor(
                utmp[:], utmp[:], ps_rep[:, 0:T], ALU.mult),
                deps=(t_u,))
            if gain_bias is not None:
                g_sb, b_sb = gain_bias
                t_u2 = b.ve(lambda v, c=c: v.tensor_scalar(
                    utmp[:], utmp[:], g_sb[:, c:c + 1],
                    b_sb[:, c:c + 1], ALU.mult, ALU.add), deps=(t_u2,))
            prev_wr = out_writer(c, utmp[:], (t_u2,))
            outs.append(prev_wr)
        ps_es2.close()
        return outs

    def ln_to_sbuf_bf(x_dram, out_bf, gain_bias=None, shift=False,
                      chunk_deps=None):
        def loader(c, dst, war_dep):
            cd = chunk_deps[c] if chunk_deps is not None else None
            if not shift or c < NCH // 2:
                return b.dq(dst, x_dram[c * 128:(c + 1) * 128, :],
                            deps=(war_dep, cd))
            tz = b.ve(lambda v: v.memset(dst[:, 0:1], 0.0), deps=(war_dep,))
            return b.dq(dst[:, 1:T], x_dram[c * 128:(c + 1) * 128, 0:T - 1],
                        deps=(tz, cd))

        def writer(c, utmp_ap, deps):
            return b.ve(lambda v: v.tensor_copy(out_bf[:, c * T:(c + 1) * T],
                                                utmp_ap), deps=deps)
        return emit_ln_core(loader, writer, gain_bias)

    def ln_to_dram_f32(x_dram, out_dram):
        def loader(c, dst, war_dep):
            return b.dq(dst, x_dram[c * 128:(c + 1) * 128, :], deps=(war_dep,))

        def writer(c, utmp_ap, deps):
            return b.dq(out_dram[c * 128:(c + 1) * 128, :], utmp_ap, deps=deps)
        return emit_ln_core(loader, writer, None)

    def project_fm(wT_dram, bias_dram, src_bf, out_bf):
        wbuf = b.sb("prj_w", [128, 2 * D], BF16)
        bsb = b.sb("prj_b", [128, NCH], F32)
        tb = b.dq(bsb.ap(), bias_dram.rearrange("(a p) -> p a", p=128), slow=True)
        ps = [b.ps("prj_ps", [128, 512]) for _ in range(4)]
        toks = []
        wtok = [None, None]
        for e in range(NCH):
            slot = e % 2
            wtok[slot] = b.dg(
                wbuf[:, slot * D:(slot + 1) * D].rearrange(
                    "p (c e) -> p c e", c=NCH),
                wT_dram[:, e * 128:(e + 1) * 128].rearrange(
                    "(c p) e -> p c e", p=128),
                deps=(toks[4 * (e - 2) + 3] if e >= 2 else None,))
            for h in range(NT4):
                pst = ps[(e * NT4 + h) % 4]
                mt = None
                for c in range(NCH):
                    mt = b.mm(pst[:],
                              wbuf[:, slot * D + c * 128: slot * D + (c + 1) * 128],
                              src_bf[:, c * T + h * 512: c * T + h * 512 + 512],
                              start=(c == 0), stop=(c == NCH - 1),
                              deps=(wtok[slot],
                                    toks[4 * e + h - 4] if 4 * e + h >= 4
                                    else None))
                toks.append(b.se(
                    lambda s, e=e, h=h, pst=pst: s.activation(
                        out_bf[:, e * T + h * 512: e * T + h * 512 + 512],
                        pst[:], AF.Identity, bias=bsb[:, e:e + 1]),
                    deps=(mt, tb)))
        return toks

    def project_tm(wT_dram, bias_dram, src_bf, out_tm):
        wbuf = b.sb("pv_w", [128, 2 * NCH * 512], BF16)
        brow = b.sb("pv_brow", [1, D], F32)
        brep = b.sb("pv_brepsb", [128, D], F32)
        ps_brep = b.ps("pv_brep", [128, D])
        tb = b.dq(brow[:], bias_dram.unsqueeze(0))
        btoks = [b.mm(ps_brep[:, h * 512:(h + 1) * 512], cT["ones_f"][:],
                      brow[:, h * 512:(h + 1) * 512], start=True, stop=True,
                      deps=(tb,)) for h in range(2)]
        btoks = [b.se(lambda s: s.activation(brep[:], ps_brep[:], AF.Identity),
                      deps=tuple(btoks))]
        ps = [b.ps("pv_ps", [128, 512]) for _ in range(4)]
        toks = []
        wtok = [None, None]
        for e2 in range(2):
            slot = e2 % 2
            wtok[slot] = b.dg(
                wbuf[:, slot * NCH * 512:(slot + 1) * NCH * 512].rearrange(
                    "p (c e) -> p c e", c=NCH),
                wT_dram[:, e2 * 512:(e2 + 1) * 512].rearrange(
                    "(c p) e -> p c e", p=128),
                deps=(toks[-8] if len(toks) >= 8 else None,))
            for tc in range(16):
                pst = ps[tc % 4]
                mt = None
                for c in range(NCH):
                    mt = b.mm(pst[:],
                              src_bf[:, c * T + tc * 128: c * T + (tc + 1) * 128],
                              wbuf[:, slot * NCH * 512 + c * 512:
                                   slot * NCH * 512 + (c + 1) * 512],
                              start=(c == 0), stop=(c == NCH - 1),
                              deps=(wtok[slot],
                                    toks[-4] if len(toks) >= 4 else None))
                toks.append(b.ve(
                    lambda v, tc=tc, e2=e2, pst=pst: v.tensor_tensor(
                        out_tm[:, tc * D + e2 * 512: tc * D + (e2 + 1) * 512],
                        pst[:], brep[:, e2 * 512:(e2 + 1) * 512], ALU.add),
                    deps=(mt,) + tuple(btoks)))
        return toks

    def attn_causal(kind, x_src, x_dst):
        pre = kind
        with b.phase():
            Qf = b.sb(pre + "_Q", [128, NCH * T], BF16)
            Kf = b.sb(pre + "_K", [128, NCH * T], BF16)
            Vt = b.sb(pre + "_V", [128, 16 * D], BF16)
            with b.phase():
                xhat = b.sb(pre + "_xhat", [128, NCH * T], BF16)
                with b.phase():
                    ln_to_sbuf_bf(x_src, xhat)
                with b.phase():
                    project_fm(dr[pre + "WqT"], dr[pre + "_bq"], xhat, Qf)
                with b.phase():
                    project_fm(dr[pre + "WkT"], dr[pre + "_bk"], xhat, Kf)
                with b.phase():
                    project_tm(dr[pre + "WvT"], dr[pre + "_bv"], xhat, Vt)
            if kind == "b":
                aout = b.sb("b_ao", [128, 16 * D], BF16)

            iscale = 1.0 / 32.0 if kind == "a" else 1.0 / 8.0
            xtile = b.sb(pre + "_xt", [128, 2 * 512], F32)
            rtile = b.sb(pre + "_rt", [128, 2 * 512], F32)
            att_ps = contextlib.ExitStack()

            def asb(stem, shape, dtype):
                return att_ps.enter_context(
                    b.nc.sbuf_tensor(b.name(stem), shape, dtype))

            if kind == "a":
                psA = [att_ps.enter_context(b.nc.psum_tensor(
                    b.name(pre + "_S"), [128, 512], F32)) for _ in range(3)]
                psA_tok = [None, None, None]
                psA_rr = [0]
                S_sb = [asb(pre + "_Ssb", [128, T], F32) for _ in range(2)]
            else:
                psS = [att_ps.enter_context(b.nc.psum_tensor(
                    b.name(pre + "_Sh"), [128, 1024], F32)) for _ in range(3)]
                psS_tok = [None, None, None]
                psS_rr = [0]
                S_sb = [asb(pre + "_Ssb", [128, T], F32) for _ in range(2)]
                R_sb = asb(pre + "_R", [128, T], F32)
            t32 = [asb(pre + "_t32", [128, 32], F32) for _ in range(2)]
            nmax = [asb(pre + "_nm", [128, 1], F32) for _ in range(2)]
            zcol = [asb(pre + "_z", [128, 1], F32) for _ in range(2)]
            rcol = [asb(pre + "_r", [128, 1], F32) for _ in range(2)]
            lcol = [asb(pre + "_l", [128, 1], F32) for _ in range(2)]
            Pb = [asb(pre + "_P", [128, T], BF16) for _ in range(2)]
            Pm = [asb(pre + "_Pm", [128, T], BF16) for _ in range(2)]
            PT = asb(pre + "_PT", [128, 16 * 512], BF16)
            ps_av = [att_ps.enter_context(b.nc.psum_tensor(
                b.name(pre + "_av"), [128, 512], F32)) for _ in range(2)]

            prev_sel = [None]
            prev_tr = [None]
            pS_read = [None, None]
            pS_exp = [None, None]
            pPm_tr = [None, None]
            pZrd = [None, None]
            prev_pm = [None]
            prev_exp = [None]

            pNm = [None, None]
            pZv = [None, None]
            pPb_rd = [None, None]

            def head_a(qt):
                W = 128 * (qt + 1)
                p = qt % 2
                ev = None
                for w0 in range(0, W, 512):
                    w1 = min(W, w0 + 512)
                    last = (w1 == W)
                    sl = psA_rr[0] % 3
                    psA_rr[0] += 1
                    mt = None
                    for c in range(NCH):
                        mt = b.mm(psA[sl][:, 0:w1 - w0],
                                  Qf[:, c * T + qt * 128:
                                     c * T + (qt + 1) * 128],
                                  Kf[:, c * T + w0: c * T + w1],
                                  start=(c == 0),
                                  stop=(c == NCH - 1 and not last),
                                  deps=(psA_tok[sl],))
                    if last:
                        mt = b.mm(psA[sl][:, W - 128 - w0:W - w0],
                                  c_mdT_bf[:], c_id_bf[:],
                                  start=False, stop=True, deps=(mt,),
                                  tile_position=(0, 0))
                    ev = b.se(lambda s, sl=sl, w0=w0, w1=w1, p=p:
                              s.activation(S_sb[p][:, w0:w1],
                                           psA[sl][:, 0:w1 - w0],
                                           AF.Identity),
                              deps=(mt, pS_read[p], ev))
                    psA_tok[sl] = ev
                return ev

            def tail_a_A(qt, ev):
                W = 128 * (qt + 1)
                p = qt % 2
                t_max = b.ve(lambda v, W=W, p=p: v.max(
                    out=t32[p][:, 0:8], in_=S_sb[p][:, 0:W]),
                    deps=(ev, pNm[p]) + tuple(init))
                t_nm = b.se(lambda s, p=p: s.activation(
                    nmax[p][:], t32[p][:, 0:1], AF.Identity, scale=-iscale),
                    deps=(t_max, pS_exp[p]))
                pNm[p] = t_nm
                t_exp = b.se(lambda s, W=W, p=p: s.activation(
                    Pb[p][:, 0:W], S_sb[p][:, 0:W], AF.Exp, bias=nmax[p][:],
                    scale=iscale, accum_out=zcol[p][:]),
                    deps=(t_nm, pPb_rd[p], pZv[p]))
                pS_exp[p] = t_exp
                pS_read[p] = t_exp
                return t_exp

            def tail_a_B(qt, qslot, t_exp):
                W = 128 * (qt + 1)
                p = qt % 2
                t_r = b.ve(lambda v, p=p: v.reciprocal(rcol[p][:], zcol[p][:]),
                           deps=(t_exp,))
                pZv[p] = t_r
                t_pm = b.ve(lambda v, W=W, p=p: v.tensor_scalar(
                    Pm[p][:, 0:W], Pb[p][:, 0:W], rcol[p][:], None, ALU.mult),
                    deps=(t_r, pPm_tr[p]))
                pPb_rd[p] = t_pm
                t_tr = b.dq(PT.ap().rearrange("p (c j) -> p c j", j=512)
                            [:, 0:W // 128, qslot * 128:(qslot + 1) * 128],
                            Pm[p][:, 0:W], deps=(t_pm, prev_tr[0]),
                            transpose=True)
                pPm_tr[p] = t_tr
                return t_tr

            def head_b(h, qt):
                W = 128 * (qt + 1)
                p = qt % 2
                hb, hoff = h // 2, 64 * (h % 2)
                ev = None
                for a0 in range(0, W, 1024):
                    a1 = min(W, a0 + 1024)
                    sl = psS_rr[0] % 3
                    psS_rr[0] += 1
                    mt = None
                    for w0 in range(a0, a1, 512):
                        w1 = min(a1, w0 + 512)
                        last = (w1 == W)
                        mt = b.mm(psS[sl][:, w0 - a0:w1 - a0],
                                  Qf[:, hb * T + qt * 128:
                                     hb * T + (qt + 1) * 128][hoff:hoff + 64, :],
                                  Kf[:, hb * T + w0: hb * T + w1]
                                  [hoff:hoff + 64, :],
                                  start=True, stop=not last,
                                  deps=(psS_tok[sl],))
                        if last:
                            mt = b.mm(psS[sl][:, W - 128 - a0:W - a0],
                                      c_mdT_bf[:], c_id_bf[:],
                                      start=False, stop=True, deps=(mt,),
                                      tile_position=(0, 0))
                    ev = b.se(lambda s, sl=sl, a0=a0, a1=a1, p=p: s.activation(
                        S_sb[p][:, a0:a1], psS[sl][:, 0:a1 - a0], AF.Identity),
                        deps=(mt, pS_read[p], ev))
                    psS_tok[sl] = ev
                return ev

            pNm = [None, None]

            def tail_b(h, qt, qslot, ev):
                W = 128 * (qt + 1)
                p = qt % 2
                r1 = b.ve(lambda v, W=W, p=p: v.max(out=t32[p][:, 0:8],
                                                    in_=S_sb[p][:, 0:W]),
                          deps=(ev, pNm[p]) + tuple(init))
                t_nm = b.se(lambda s, p=p: s.activation(
                    nmax[p][:], t32[p][:, 0:1], AF.Identity, scale=-iscale),
                    deps=(r1, pS_exp[p]))
                pNm[p] = t_nm
                t_exp = b.se(lambda s, W=W, p=p: s.activation(
                    Pb[p][:, 0:W], S_sb[p][:, 0:W], AF.Exp, bias=nmax[p][:],
                    scale=iscale), deps=(t_nm,))
                pS_exp[p] = t_exp
                tsel = r1
                for r in range(1, 4):
                    tsel = b.ve(lambda v, W=W, p=p, r=r: v.match_replace(
                        out=R_sb[:, 0:W],
                        in_to_replace=t32[p][:, 8 * (r - 1):8 * r],
                        in_values=(S_sb[p] if r == 1 else R_sb)[:, 0:W],
                        imm_value=NEG), deps=(tsel,))
                    tsel = b.ve(lambda v, W=W, p=p, r=r: v.max(
                        out=t32[p][:, 8 * r:8 * r + 8], in_=R_sb[:, 0:W]),
                        deps=(tsel,))
                t_stt = b.ve(lambda v, W=W, p=p: v.scalar_tensor_tensor(
                    Pm[p][:, 0:W], S_sb[p][:, 0:W], t32[p][:, 31:32],
                    Pb[p][:, 0:W], ALU.is_ge, ALU.mult, accum_out=zcol[p][:]),
                    deps=(t_exp, tsel, pPm_tr[p]))
                pS_read[p] = t_stt
                t_r = b.ve(lambda v, p=p: v.reciprocal(rcol[p][:], zcol[p][:]),
                           deps=(t_stt,))
                t_pm = b.ve(lambda v, W=W, p=p: v.tensor_scalar(
                    Pm[p][:, 0:W], Pm[p][:, 0:W], rcol[p][:], None, ALU.mult),
                    deps=(t_r,))
                t_tr = b.dq(PT.ap().rearrange("p (c j) -> p c j", j=512)
                            [:, 0:W // 128, qslot * 128:(qslot + 1) * 128],
                            Pm[p][:, 0:W], deps=(t_pm, prev_tr[0]),
                            transpose=True)
                pPm_tr[p] = t_tr
                return t_tr

            res_toks = []
            pend_evac = [None]

            def emit_pend():
                if pend_evac[0] is not None:
                    pend_evac[0]()
                    pend_evac[0] = None

            def flush_av(h, qb, tr_toks):
                kcmax = 4 * (qb + 1)
                if kind == "a":
                    last = None
                    for c in range(NCH):
                        pst = ps_av[c % 2]
                        tsl = c % 2
                        xt = xtile[:, tsl * 512:(tsl + 1) * 512]
                        rt = rtile[:, tsl * 512:(tsl + 1) * 512]
                        mt = None
                        for kc in range(kcmax):
                            mt = b.mm(
                                pst[:],
                                Vt.ap().rearrange("p (tc e) -> p tc e", tc=16)
                                [:, kc, c * 128:(c + 1) * 128],
                                PT[:, kc * 512:(kc + 1) * 512],
                                start=(kc == 0), stop=(kc == kcmax - 1),
                                deps=tuple(tr_toks) +
                                (res_toks[-2] if len(res_toks) >= 2 else None,))
                        lx = b.dq(xt,
                                  x_src[c * 128:(c + 1) * 128,
                                        qb * 512:(qb + 1) * 512],
                                  deps=(res_toks[-2] if len(res_toks) >= 2
                                        else None,))
                        ta = b.ve(lambda v, pst=pst, xt=xt, rt=rt:
                                  v.tensor_tensor(rt, pst[:], xt, ALU.add),
                            deps=(mt, lx,
                                  res_toks[-2] if len(res_toks) >= 2 else None))
                        st = b.dq(x_dst[c * 128:(c + 1) * 128,
                                        qb * 512:(qb + 1) * 512],
                                  rt, deps=(ta,))
                        res_toks.append(st)
                        last = mt
                    return last
                hb, hoff = h // 2, 64 * (h % 2)
                pst = ps_av[(h * 4 + qb) % 2]
                mt = None
                for kc in range(kcmax):
                    mt = b.mm(pst[hoff:hoff + 64, :],
                              Vt.ap().rearrange("p (tc e) -> p tc e", tc=16)
                              [:, kc, h * DH:(h + 1) * DH],
                              PT[:, kc * 512:(kc + 1) * 512],
                              start=(kc == 0), stop=(kc == kcmax - 1),
                              deps=tuple(tr_toks) +
                              (res_toks[-2] if len(res_toks) >= 2 else None,),
                              tile_position=(0, hoff))

                def mk_evac(hb=hb, hoff=hoff, qb=qb, pst=pst, mt=mt):
                    tk = b.se(lambda s: s.activation(
                        aout[:, hb * T + qb * 512:hb * T + (qb + 1) * 512]
                        [hoff:hoff + 64, :],
                        pst[hoff:hoff + 64, :], AF.Identity),
                        deps=(mt, res_toks[-1] if res_toks else None))
                    res_toks.append(tk)
                pend_evac[0] = mk_evac
                return mt

            def pt_fill(qb):
                zs = []
                for j in range(4):
                    qt = qb * 4 + j
                    for kc in range(qt + 1, 4 * (qb + 1)):
                        zs.append(b.ge(
                            lambda g, kc=kc, j=j: g.memset(
                                PT[:, kc * 512 + j * 128:
                                   kc * 512 + (j + 1) * 128], 0.0),
                            deps=(prev_tr[0],)))
                return zs

            if kind == "a":
                trs_map = {}
                ev_next = head_a(0)
                pB = None
                for qt in range(16):
                    qb, j = qt // 4, qt % 4
                    ev = ev_next
                    if qt + 1 < 16:
                        ev_next = head_a(qt + 1)
                    if j == 0:
                        trs_map[qb] = pt_fill(qb)
                    te = tail_a_A(qt, ev)
                    if pB is not None:
                        pqt, pte = pB
                        tr = tail_a_B(pqt, pqt % 4, pte)
                        trs_map[pqt // 4].append(tr)
                        if pqt % 4 == 3:
                            prev_tr[0] = flush_av(0, pqt // 4,
                                                  trs_map.pop(pqt // 4))
                    pB = (qt, te)
                pqt, pte = pB
                tr = tail_a_B(pqt, pqt % 4, pte)
                trs_map[pqt // 4].append(tr)
                prev_tr[0] = flush_av(0, pqt // 4, trs_map.pop(pqt // 4))
                b.barrier()
                att_ps.close()
            else:
                units = [(h, qb, j) for h in range(NH_B) for qb in range(4)
                         for j in range(4)]
                trs_map = {}
                ev_next = head_b(units[0][0], units[0][1] * 4 + units[0][2])
                for n, (h, qb, j) in enumerate(units):
                    ev = ev_next
                    if n + 1 < len(units):
                        nh_, nqb_, nj_ = units[n + 1]
                        ev_next = head_b(nh_, nqb_ * 4 + nj_)
                    if j == 0:
                        trs_map[(h, qb)] = pt_fill(qb)
                    tr = tail_b(h, qb * 4 + j, j, ev)
                    trs_map[(h, qb)].append(tr)
                    emit_pend()
                    if j == 3:
                        prev_tr[0] = flush_av(h, qb, trs_map.pop((h, qb)))
                emit_pend()
            b.barrier()
            att_ps.close()

            if kind == "b":
                wbuf = b.sb("bo_w", [128, 2 * D], BF16)
                bsb = b.sb("bo_b", [128, NCH], F32)
                tb = b.dq(bsb.ap(), dr["b_bo"].rearrange("(a p) -> p a", p=128), slow=True)
                ps2 = [b.ps("bo_ps", [128, 512]) for _ in range(2)]
                toks2 = []
                wtok = [None, None]
                for e in range(NCH):
                    slot = e % 2
                    wtok[slot] = b.dg(
                        wbuf[:, slot * D:(slot + 1) * D].rearrange(
                            "p (c e) -> p c e", c=NCH),
                        dr["bWoT"][:, e * 128:(e + 1) * 128].rearrange(
                            "(c p) e -> p c e", p=128),
                        deps=(toks2[4 * (e - 2) + 3] if e >= 2 else None,))
                    for h in range(NT4):
                        pst = ps2[(e * NT4 + h) % 2]
                        mt = None
                        for c in range(NCH):
                            mt = b.mm(
                                pst[:],
                                wbuf[:, slot * D + c * 128:
                                     slot * D + (c + 1) * 128],
                                aout[:, c * T + h * 512: c * T + h * 512 + 512],
                                start=(c == 0), stop=(c == NCH - 1),
                                deps=(wtok[slot],
                                      toks2[-2] if len(toks2) >= 2 else None))
                        tsl = (e * NT4 + h) % 2
                        xt = xtile[:, tsl * 512:(tsl + 1) * 512]
                        rt = rtile[:, tsl * 512:(tsl + 1) * 512]
                        lx = b.dq(xt,
                                  x_src[e * 128:(e + 1) * 128,
                                        h * 512:(h + 1) * 512],
                                  deps=(toks2[-2] if len(toks2) >= 2 else None,))
                        ta = b.ve(lambda v, pst=pst, e=e, xt=xt, rt=rt:
                                  v.scalar_tensor_tensor(
                                      rt, pst[:], bsb[:, e:e + 1], xt,
                                      ALU.add, ALU.add),
                            deps=(mt, lx, tb,
                                  toks2[-2] if len(toks2) >= 2 else None))
                        toks2.append(b.dq(
                            x_dst[e * 128:(e + 1) * 128, h * 512:(h + 1) * 512],
                            rt, deps=(ta,)))

    def ffn_inner(lidx, f_src, f_mid, dst_pair, stop_after_c=False):
        fes = contextlib.ExitStack()
        fes.enter_context(b.phase())
        obf = b.sb("fl_obf", [128, 4 * T], BF16)
        wo_sb = b.sb("fl_wo", [128, 4 * D], BF16)
        with b.phase():
            qf_bf = b.sb("fl_qf", [128, 4 * T], BF16)
            kf_bf = b.sb("fl_kf", [128, 4 * T], BF16)
            v_tm = b.sb("fl_vtm", [128, 16 * INNER], BF16)
            with b.phase():
                wqkv = b.sb("fl_wqkv", [128, NCH * 3 * INNER], BF16)
                w_toks = [b.dg(wqkv[:, c * 3 * INNER:(c + 1) * 3 * INNER],
                               dr["f_wqkv"][lidx, c * 128:(c + 1) * 128, :])
                          for c in range(NCH)]
                hbf = b.sb("fl_h", [128, NCH * T], BF16)
                h_toks = []
                for c in range(NCH):
                    if c < NCH // 2:
                        h_toks.append(b.dg(hbf[:, c * T:(c + 1) * T],
                                           f_src[c * 128:(c + 1) * 128, :]))
                    else:
                        t0 = b.ve(lambda v, c=c: v.memset(
                            hbf[:, c * T:c * T + 1], 0.0))
                        h_toks.append(b.dg(hbf[:, c * T + 1:(c + 1) * T],
                                           f_src[c * 128:(c + 1) * 128, 0:T - 1],
                                           deps=(t0,)))
                qs_rep = b.sb("fl_qsrep", [128, 4 * DH], F32)
                qrow = b.sb("fl_qrow", [1, 4 * DH], F32)
                tl = [b.dq(qrow[:, 0:DH], dr["f_qs"][lidx].unsqueeze(0)),
                      b.dq(qrow[:, DH:2 * DH], dr["f_qs_rot"][lidx].unsqueeze(0)),
                      b.dq(qrow[:, 2 * DH:3 * DH], dr["f_ks"][lidx].unsqueeze(0)),
                      b.dq(qrow[:, 3 * DH:4 * DH],
                           dr["f_ks_rot"][lidx].unsqueeze(0))]
                ps_qs = b.ps("fl_qsps", [128, 4 * DH])
                t_qs = b.mm(ps_qs[:], cT["ones_f"][:], qrow[:], start=True,
                            stop=True, deps=tuple(tl))
                t_qs2 = b.ve(lambda v: v.tensor_copy(qs_rep[:], ps_qs[:]),
                             deps=(t_qs,))
                Dt = {}
                d_toks = []
                for pr, c1, c2, off in (("q", "c1q", "c2q", 0),
                                        ("k", "c1k", "c2k", 2 * DH)):
                    D1 = b.sb("fl_D1" + pr, [128, 16 * DH], F32)
                    D2 = b.sb("fl_D2" + pr, [128, 16 * DH], F32)
                    Dt[pr] = (D1, D2)
                    d_toks.append(b.ve(lambda v, c1=c1, D1=D1, off=off:
                                       v.tensor_tensor(
                        D1.ap().rearrange("p (a x i) -> p a x i", a=16, x=2),
                        cT[c1].ap().rearrange("p (a i) -> p a i", a=16)
                        .unsqueeze(2).to_broadcast([128, 16, 2, 32]),
                        qs_rep[:, off:off + DH]
                        .rearrange("p (x i) -> p x i", x=2)
                        .unsqueeze(1).to_broadcast([128, 16, 2, 32]),
                        ALU.mult), deps=(t_qs2,)))
                    d_toks.append(b.ve(lambda v, c2=c2, D2=D2, off=off:
                                       v.scalar_tensor_tensor(
                        D2.ap().rearrange("p (a x i) -> p a x i", a=16, x=2)
                        [:, :, 0, :],
                        cT[c2].ap().rearrange("p (a i) -> p a i", a=16), -1.0,
                        qs_rep[:, off + DH:off + DH + 32].unsqueeze(1)
                        .to_broadcast([128, 16, 32]),
                        ALU.mult, ALU.mult), deps=(t_qs2,)))
                    d_toks.append(b.ve(lambda v, c2=c2, D2=D2, off=off:
                                       v.scalar_tensor_tensor(
                        D2.ap().rearrange("p (a x i) -> p a x i", a=16, x=2)
                        [:, :, 1, :],
                        cT[c2].ap().rearrange("p (a i) -> p a i", a=16), 1.0,
                        qs_rep[:, off + DH + 32:off + 2 * DH].unsqueeze(1)
                        .to_broadcast([128, 16, 32]),
                        ALU.mult, ALU.mult), deps=(t_qs2,)))

                qkv_ps = [b.ps("fl_qkvps", [128, 3 * INNER]) for _ in range(2)]
                qch = b.sb("fl_qch", [128, 2 * INNER], F32)
                kch = b.sb("fl_kch", [128, 2 * INNER], F32)
                sqs = b.sb("fl_sq", [128, INNER], F32)
                n2 = b.sb("fl_n2", [128, HEADS], F32)
                rr = b.sb("fl_rr", [128, HEADS], F32)
                zz = b.sb("fl_zz", [128, INNER], F32)
                t1b = b.sb("fl_t1", [128, INNER], F32)
                t2b = b.sb("fl_t2", [128, INNER], F32)
                qp = b.sb("fl_qp", [128, INNER], BF16)
                ev_toks = []
                rot_state = {}

                def rotary(chunk_ap, D1, D2, dst_fm, tc, deps):
                    st = rot_state
                    ta = b.se(lambda s: s.activation(sqs[:], chunk_ap, AF.Square),
                              deps=deps + (st.get("tb2"),))
                    tb2 = b.ve(lambda v: v.tensor_reduce(
                        n2[:], sqs.ap().rearrange("p (h i) -> p h i", h=HEADS),
                        AX.X, ALU.add), deps=(ta, st.get("tc3")))
                    tc1 = b.se(lambda s: s.activation(n2[:], n2[:], AF.Ln,
                                                      bias=cT["eps30"][:]),
                               deps=(tb2,) + tuple(init))
                    tc2 = b.se(lambda s: s.activation(n2[:], n2[:], AF.Exp,
                                                      scale=-0.5), deps=(tc1,))
                    tc3 = b.ve(lambda v: v.tensor_scalar(rr[:], n2[:], 1.0e12,
                                                         None, ALU.min),
                               deps=(tc2, st.get("tz")))
                    tz = b.ve(lambda v: v.tensor_tensor(
                        zz.ap().rearrange("p (h i) -> p h i", h=HEADS),
                        chunk_ap.rearrange("p (h i) -> p h i", h=HEADS),
                        rr.ap().unsqueeze(2).to_broadcast([128, HEADS, DH]),
                        ALU.mult), deps=(tc3, st.get("tC")))
                    tA = b.ve(lambda v: v.tensor_tensor(
                        t1b.ap().rearrange("p (h i) -> p h i", h=HEADS),
                        zz.ap().rearrange("p (h i) -> p h i", h=HEADS),
                        D1[:, tc * DH:(tc + 1) * DH].unsqueeze(1)
                        .to_broadcast([128, HEADS, DH]),
                        ALU.mult), deps=(tz, st.get("tC")) + tuple(d_toks))
                    tB0 = b.ve(lambda v: v.tensor_tensor(
                        t2b.ap().rearrange("p (h x i) -> p h x i", h=HEADS, x=2)
                        [:, :, 0, :],
                        zz.ap().rearrange("p (h x i) -> p h x i", h=HEADS, x=2)
                        [:, :, 1, :],
                        D2.ap().rearrange("p (a x i) -> p a x i", a=16, x=2)
                        [:, tc, 0, :].unsqueeze(1).to_broadcast([128, HEADS, 32]),
                        ALU.mult), deps=(tz, st.get("tC")))
                    tB1 = b.ve(lambda v: v.tensor_tensor(
                        t2b.ap().rearrange("p (h x i) -> p h x i", h=HEADS, x=2)
                        [:, :, 1, :],
                        zz.ap().rearrange("p (h x i) -> p h x i", h=HEADS, x=2)
                        [:, :, 0, :],
                        D2.ap().rearrange("p (a x i) -> p a x i", a=16, x=2)
                        [:, tc, 1, :].unsqueeze(1).to_broadcast([128, HEADS, 32]),
                        ALU.mult), deps=(tz,))
                    tC = b.ve(lambda v: v.tensor_tensor(qp[:], t1b[:], t2b[:],
                                                        ALU.add),
                              deps=(tA, tB0, tB1, st.get("ttr")))
                    ttr = b.dq(dst_fm.ap().rearrange("p (c j) -> p c j", j=T)
                               [:, :, tc * 128:(tc + 1) * 128], qp[:],
                               deps=(tC, st.get("ttr")), transpose=True)
                    st.update(tb2=tb2, tc3=tc3, tz=tz, tA=tA, tC=tC, ttr=ttr)
                    return ttr

                for tc in range(16):
                    pst = qkv_ps[tc % 2]
                    qslot = tc % 2
                    mt = None
                    for c in range(NCH):
                        for sp in range(3):
                            mt = b.mm(pst[:, sp * 512:(sp + 1) * 512],
                                      hbf[:, c * T + tc * 128:
                                          c * T + (tc + 1) * 128],
                                      wqkv[:, c * 3 * INNER + sp * 512:
                                           c * 3 * INNER + (sp + 1) * 512],
                                      start=(c == 0), stop=(c == NCH - 1),
                                      deps=(w_toks[c], h_toks[c],
                                            ev_toks[-3] if len(ev_toks) >= 3
                                            else None))
                    e_q = b.se(lambda s, pst=pst, qslot=qslot: s.activation(
                        qch[:, qslot * INNER:(qslot + 1) * INNER],
                        pst[:, 0:INNER], AF.Identity),
                        deps=(mt, rot_state.get("tz")))
                    e_k = b.se(lambda s, pst=pst, qslot=qslot: s.activation(
                        kch[:, qslot * INNER:(qslot + 1) * INNER],
                        pst[:, INNER:2 * INNER], AF.Identity),
                        deps=(mt, rot_state.get("tz")))
                    e_v = b.se(lambda s, pst=pst, tc=tc: s.activation(
                        v_tm[:, tc * INNER:(tc + 1) * INNER],
                        pst[:, 2 * INNER:3 * INNER], AF.Identity), deps=(mt,))
                    ev_toks += [e_q, e_k, e_v]
                    rotary(qch[:, qslot * INNER:(qslot + 1) * INNER],
                           Dt["q"][0], Dt["q"][1], qf_bf, tc, (e_q,))
                    rotary(kch[:, qslot * INNER:(qslot + 1) * INNER],
                           Dt["k"][0], Dt["k"][1], kf_bf, tc, (e_k,))

            with b.phase():
                Ptile = [b.sb("fl_P", [128, 8 * 512], BF16) for _ in range(2)]
                pz_init = [b.ve(lambda v, pp=pp: v.memset(pp[:], 0.0))
                           for pp in Ptile]
                ecol = b.sb("fl_ecol", [128, 64], BF16)
                ec0 = b.ve(lambda v: v.memset(ecol[:], 0.0))
                ec_toks = [b.ve(lambda v, h=h: v.memset(
                    ecol[:, 8 * h + h:8 * h + h + 1], 1.0), deps=(ec0,))
                    for h in range(HEADS)]
                att_es = contextlib.ExitStack()
                ps_sim = [att_es.enter_context(b.nc.psum_tensor(
                    b.name("fl_sim"), [128, 512], F32)) for _ in range(4)]
                sim_tok = [None] * 4
                sim_rr = [0]
                ps_z8 = att_es.enter_context(b.nc.psum_tensor(
                    b.name("fl_z8"), [8, 512], F32))
                ps_o = [att_es.enter_context(b.nc.psum_tensor(
                    b.name("fl_o"), [128, 512], F32)) for _ in range(2)]
                zrow = b.sb("fl_zrow", [HEADS, T], F32)
                rinv = b.sb("fl_rinv", [HEADS, T], F32)
                o_fm = b.sb("fl_ofm", [128, 4 * T], F32)
                wo_toks = [b.dg(wo_sb[:, ic * D:(ic + 1) * D],
                                dr["f_wo"][lidx, ic * 128:(ic + 1) * 128, :])
                           for ic in range(4)]
                prev_o = {(0, 0): None, (64, 0): None, (0, 1): None, (64, 1): None}
                pt_last = pz_init[:]
                att_toks = []
                zev_toks = []

                def tile_cls(kt, w):
                    out = []
                    for qsub in range(4):
                        if w == 0:
                            if kt < 4:
                                out.append((qsub, "skip"))
                                continue
                            rel = 128 * (kt - 4)
                        else:
                            rel = 128 * kt - 512
                        delta = 128 * qsub - rel
                        if delta in (128, 256, 384):
                            out.append((qsub, "full"))
                        elif delta == 0:
                            out.append((qsub, "low"))
                        elif delta == 512:
                            out.append((qsub, "up"))
                        else:
                            out.append((qsub, "skip"))
                    return out

                for w in range(4):
                    zt = None
                    for h in range(HEADS):
                        hb, ho = h // 2, 64 * (h % 2)
                        q0 = w * 512
                        pslot = (w * HEADS + h) % 2
                        PtS = Ptile[pslot]
                        ktile_valid = []
                        ptoks = []
                        for kt in range(8):
                            cls = tile_cls(kt, w)
                            spans = [qs for qs, kd in cls if kd != "skip"]
                            if not spans:
                                continue
                            if w == 0:
                                ktc = kt - 4
                            else:
                                ktc = 4 * (w - 1) + kt
                            ktile_valid.append((kt, ktc))
                            sl = sim_rr[0] % 4
                            sim_rr[0] += 1
                            s0, s1 = min(spans) * 128, (max(spans) + 1) * 128
                            mt = b.mm(ps_sim[sl][:, s0:s1],
                                      kf_bf[:, hb * T + ktc * 128:
                                            hb * T + (ktc + 1) * 128]
                                      [ho:ho + 64, :],
                                      qf_bf[:, hb * T + q0 + s0:
                                            hb * T + q0 + s1][ho:ho + 64, :],
                                      start=True, stop=True,
                                      deps=(sim_tok[sl],))
                            mtoks = [mt]
                            for qsub, kd in cls:
                                if kd in ("low", "up"):
                                    mtbl = c_mlow if kd == "low" else c_mup
                                    mtoks.append(b.ve(
                                        lambda v, sl=sl, qsub=qsub, mtbl=mtbl:
                                        v.tensor_tensor(
                                            ps_sim[sl][:, qsub * 128:
                                                       (qsub + 1) * 128],
                                            ps_sim[sl][:, qsub * 128:
                                                       (qsub + 1) * 128],
                                            mtbl[:], ALU.add),
                                        deps=(mt,) + tuple(init)))
                            pe = b.se(lambda s, kt=kt, sl=sl, s0=s0, s1=s1,
                                      PtS=PtS: s.activation(
                                          PtS[:, kt * 512 + s0:kt * 512 + s1],
                                          ps_sim[sl][:, s0:s1], AF.Exp),
                                      deps=tuple(mtoks) + (pt_last[pslot],))
                            sim_tok[sl] = pe
                            ptoks.append(pe)
                        zt = None
                        for i, (kt, ktc) in enumerate(ktile_valid):
                            zt = b.mm(ps_z8[:], ecol[:, 8 * h:8 * h + 8],
                                      PtS[:, kt * 512:(kt + 1) * 512],
                                      start=(h == 0 and i == 0),
                                      stop=(h == HEADS - 1 and
                                            i == len(ktile_valid) - 1),
                                      deps=tuple(ptoks) + tuple(ec_toks) +
                                      (zev_toks[-1] if zev_toks else None,))
                        oslot = (w * HEADS + h) % 2
                        avt = None
                        for i, (kt, ktc) in enumerate(ktile_valid):
                            avt = b.mm(ps_o[oslot][ho:ho + 64, :],
                                       v_tm.ap().rearrange(
                                           "p (tc e) -> p tc e", tc=16)
                                       [:, ktc, h * DH:(h + 1) * DH],
                                       PtS[:, kt * 512:(kt + 1) * 512],
                                       start=(i == 0),
                                       stop=(i == len(ktile_valid) - 1),
                                       deps=tuple(ptoks) +
                                       (prev_o[(ho, oslot)],),
                                       tile_position=(0, ho))
                        oev = b.se(lambda s, hb=hb, ho=ho, w=w, oslot=oslot:
                                   s.activation(
                                       o_fm[:, hb * T + w * 512:
                                            hb * T + (w + 1) * 512]
                                       [ho:ho + 64, :],
                                       ps_o[oslot][ho:ho + 64, :], AF.Identity),
                                   deps=(avt,))
                        prev_o[(ho, oslot)] = oev
                        pt_last[pslot] = avt
                        att_toks.append(oev)
                    zev = b.se(lambda s, w=w: s.activation(
                        zrow[:, w * 512:(w + 1) * 512], ps_z8[:], AF.Identity),
                        deps=(zt,))
                    zev_toks.append(zev)
                    t_ri = b.ve(lambda v, w=w: v.reciprocal(
                        rinv[:, w * 512:(w + 1) * 512],
                        zrow[:, w * 512:(w + 1) * 512]),
                        deps=(zev,) + tuple(att_toks[-HEADS:]))
                    for cb in range(4):
                        sl = sim_rr[0] % 4
                        sim_rr[0] += 1
                        tmm = b.mm(ps_sim[sl][:],
                                   c_e8[:, cb * 128:(cb + 1) * 128],
                                   rinv[:, w * 512:(w + 1) * 512],
                                   start=True, stop=True,
                                   deps=(t_ri, sim_tok[sl]))
                        sim_tok[sl] = b.ve(
                            lambda v, cb=cb, w=w, sl=sl: v.tensor_tensor(
                                obf[:, cb * T + w * 512:
                                    cb * T + (w + 1) * 512],
                                o_fm[:, cb * T + w * 512:
                                     cb * T + (w + 1) * 512],
                                ps_sim[sl][:], ALU.mult), deps=(tmm,))
                b.barrier()
                att_es.close()
                if stop_after_c:
                    wo_emit_body()

            def wo_emit_body(stats=None):
                ps_wo = [b.ps("fl_wops", [128, 512]) for _ in range(2)]
                ftile = b.sb("fl_ft", [128, 2 * 512], F32)
                fnew = b.sb("fl_fn", [128, 2 * 512], F32)
                wo_out = []
                rd_last = [None, None]
                sq_rd = [None, None]
                loop = ([(e, hh) for e in range(NCH) for hh in range(NT4)]
                        if stats is None else
                        [(e, hh) for hh in range(NT4) for e in range(NCH)])
                for i2, (e, hh) in enumerate(loop):
                    pst = ps_wo[i2 % 2]
                    tsl = i2 % 2
                    ft = ftile[:, tsl * 512:(tsl + 1) * 512]
                    fn = fnew[:, tsl * 512:(tsl + 1) * 512]
                    mt = None
                    for ic in range(4):
                        mt = b.mm(pst[:],
                                  wo_sb[:, ic * D + e * 128:
                                        ic * D + (e + 1) * 128],
                                  obf[:, ic * T + hh * 512:
                                      ic * T + (hh + 1) * 512],
                                  start=(ic == 0), stop=(ic == 3),
                                  deps=(wo_toks[ic],
                                        wo_out[-2] if len(wo_out) >= 2
                                        else None))
                    lf = b.dg(ft,
                              f_src[e * 128:(e + 1) * 128,
                                    hh * 512:(hh + 1) * 512],
                              deps=(wo_out[-2] if len(wo_out) >= 2
                                    else None,))
                    ta = b.ve(lambda v, pst=pst, ft=ft, fn=fn:
                              v.tensor_tensor(fn, pst[:], ft, ALU.add),
                        deps=(mt, lf,
                              wo_out[-2] if len(wo_out) >= 2 else None) +
                        tuple(rd_last[tsl] or ()))
                    st = b.dq(f_mid[e * 128:(e + 1) * 128,
                                    hh * 512:(hh + 1) * 512],
                              fn, deps=(ta,))
                    wo_out.append(st)
                    if stats is None:
                        rd_last[tsl] = (st,)
                        continue
                    q_ps, q_tok, srow2, sq_bf, fmid_bf, shifted, tcb_prev \
                        = stats
                    span = slice(e * T + hh * 512, e * T + (hh + 1) * 512)
                    if shifted and e >= NCH // 2:
                        wd = 511 if hh == NT4 - 1 else 512
                        d0 = e * T + hh * 512 + 1
                        if hh == 0:
                            tz0 = b.ve(lambda v, e=e: v.memset(
                                fmid_bf[:, e * T:e * T + 1], 0.0), deps=(ta,))
                        else:
                            tz0 = None
                        tcb = b.se(lambda s, fn=fn, d0=d0, wd=wd:
                                   s.activation(fmid_bf[:, d0:d0 + wd],
                                                fn[:, 0:wd], AF.Identity),
                                   deps=(ta, tz0))
                    else:
                        tcb = b.se(lambda s, fn=fn, span=span: s.activation(
                            fmid_bf[:, span], fn, AF.Identity), deps=(ta,))
                    tsq = b.se(lambda s, span=span, tsl=tsl, fmid_bf=fmid_bf:
                               s.activation(
                        sq_bf[:, tsl * 512:(tsl + 1) * 512],
                        fmid_bf[:, span], AF.Square),
                        deps=(ta, tcb, tcb_prev.get(e), sq_rd[tsl]))
                    sl = hh % 2
                    m1 = b.mm(q_ps[sl][:, 0:512], cT["ones_bf"][:],
                              fmid_bf[:, span], start=(e == 0),
                              stop=(e == NCH - 1),
                              deps=(tcb, tcb_prev.get(e), q_tok[sl])
                              + tuple(init))
                    m2 = b.mm(q_ps[sl][:, 512:1024], cT["ones_bf"][:],
                              sq_bf[:, tsl * 512:(tsl + 1) * 512],
                              start=(e == 0), stop=(e == NCH - 1),
                              deps=(tsq, q_tok[sl]))
                    sq_rd[tsl] = m2
                    tcb_prev[e] = tcb
                    rd_last[tsl] = (st, tcb)
                    if e == NCH - 1:
                        e1 = b.ve(lambda v, sl=sl, hh=hh: v.tensor_copy(
                            srow2[:, hh * 512:(hh + 1) * 512],
                            q_ps[sl][:, 0:512]), deps=(m1, m2))
                        e2 = b.ve(lambda v, sl=sl, hh=hh: v.tensor_copy(
                            srow2[:, T + hh * 512:T + (hh + 1) * 512],
                            q_ps[sl][:, 512:1024]), deps=(e1,))
                        q_tok[sl] = e2
                return wo_out

        if stop_after_c:
            fes.close()
            return
        with b.phase():
            gsb = b.sb("fl_g", [128, NCH], F32)
            bsb = b.sb("fl_b", [128, NCH], F32)
            tg = b.dq(gsb.ap(), dr["f_lng"][lidx].rearrange("(a p) -> p a", p=128), slow=True)
            tb = b.dq(bsb.ap(), dr["f_lnb"][lidx].rearrange("(a p) -> p a", p=128), slow=True)
            hbf2 = b.sb("fl_h2", [128, NCH * T], BF16)
            ln2_es = contextlib.ExitStack()
            ln2_es.enter_context(b.phase())
            fmid_bf = b.sb("fl_fmbf", [128, NCH * T], BF16)
            srow2 = b.sb("fl2_srow", [1, 2 * T], F32)
            vrow2 = b.sb("fl2_vrow", [1, T], F32)
            nrow2 = srow2[:, 0:T]
            shift_on = not DBG.get("noshift")
            with b.phase():
                q_ps = [b.ps("fl_lnq", [1, 2 * 512]) for _ in range(2)]
                sq_bf = b.sb("fl_sqbf", [128, 2 * 512], BF16)
                wo_emit_body(stats=(q_ps, [None, None], srow2, sq_bf,
                                    fmid_bf, shift_on, {}))
            t_m2 = b.ve(lambda v: v.scalar_tensor_tensor(
                vrow2[:], srow2[:, 0:T], 1.0 / D, srow2[:, 0:T],
                ALU.mult, ALU.mult))
            t_m3 = b.ve(lambda v: v.tensor_scalar(vrow2[:], vrow2[:], 1.0 / D,
                                                  None, ALU.mult),
                        deps=(t_m2,))
            t_var = b.ve(lambda v: v.scalar_tensor_tensor(
                vrow2[:], srow2[:, T:2 * T], 1.0 / D, vrow2[:],
                ALU.mult, ALU.subtract), deps=(t_m3,))
            t_lnv = b.se(lambda s: s.activation(vrow2[:], vrow2[:], AF.Ln,
                                                bias=cT["eps5"][0:1, :]),
                         deps=(t_var,) + tuple(init))
            t_rv = b.se(lambda s: s.activation(vrow2[:], vrow2[:], AF.Exp,
                                               scale=-0.5), deps=(t_lnv,))
            t_nmv = b.ve(lambda v: v.tensor_scalar(nrow2, srow2[:, 0:T],
                                                   -1.0 / D, None, ALU.mult),
                         deps=(t_m2, t_var))
            ps_es2 = contextlib.ExitStack()
            ps_rep2 = ps_es2.enter_context(
                b.nc.psum_tensor(b.name("fl2_rep"), [128, 2 * T], F32))
            reps2 = []
            for h in range(NT4):
                reps2.append(b.mm(ps_rep2[:, h * 512:(h + 1) * 512],
                                  cT["ones_f"][:],
                                  vrow2[:, h * 512:(h + 1) * 512], start=True,
                                  stop=True, deps=(t_rv,)))
                reps2.append(b.mm(ps_rep2[:, T + h * 512:T + (h + 1) * 512],
                                  cT["ones_f"][:],
                                  nrow2[0:1, h * 512:(h + 1) * 512],
                                  start=True, stop=True, deps=(t_nmv,)))
            utmp2 = b.sb("fl2_ut", [128, 2 * T], F32)
            pu = [None, None]
            for c in range(NCH):
                usl = c % 2
                ut = utmp2[:, usl * T:(usl + 1) * T]
                tcast = b.se(lambda s, ut=ut, c=c: s.activation(
                    ut[:, 0:T], fmid_bf[:, c * T:(c + 1) * T],
                    AF.Identity), deps=(pu[usl],))
                t_u = b.ve(lambda v, ut=ut: v.tensor_tensor(
                    ut, ut, ps_rep2[:, T:2 * T], ALU.add),
                    deps=(tcast,) + tuple(reps2))
                t_u2 = b.ve(lambda v, ut=ut: v.tensor_tensor(
                    ut, ut, ps_rep2[:, 0:T], ALU.mult), deps=(t_u,))
                if not DBG.get("nogain"):
                    t_u2 = b.ve(lambda v, ut=ut, c=c: v.tensor_scalar(
                        ut, ut, gsb[:, c:c + 1], bsb[:, c:c + 1],
                        ALU.mult, ALU.add), deps=(t_u2, tg, tb))
                pu[usl] = b.ve(lambda v, ut=ut, c=c: v.tensor_copy(
                    hbf2[:, c * T:(c + 1) * T], ut), deps=(t_u2,))
            ps_es2.close()
            ln2_es.close()
            if DBG.get("ffn_cut") == "l0ln2" and lidx == 0:
                with b.phase():
                    cp = b.sb("dbg_cp", [128, T], F32)
                    prev = None
                    for c in range(NCH):
                        t1 = b.ve(lambda v, c=c: v.tensor_copy(
                            cp[:], hbf2[:, c * T:(c + 1) * T]), deps=(prev,))
                        prev = b.dq(x_out[c * 128:(c + 1) * 128, :], cp[:],
                                    deps=(t1,))
                fes.close()
                return
            w1_sb = b.sb("fl_w1", [128, 2 * NCH * 128], BF16)
            b1_sb = b.sb("fl_b1", [128, 2 * NH1], F32)
            tb1 = []
            for sh in range(2):
                for c in range(NH1):
                    wdt = 42 if c == NH1 - 1 else 128
                    tb1.append(b.dq(
                        b1_sb[0:wdt, sh * NH1 + c:sh * NH1 + c + 1],
                        dr["f_b1"][lidx, sh * HID + c * 128:
                                   sh * HID + c * 128 + wdt].unsqueeze(1)))
            w1_es = contextlib.ExitStack()
            ps_h = [w1_es.enter_context(
                b.nc.psum_tensor(b.name("fl_hps"), [128, 512], F32))
                for _ in range(8)]
            hps_tok = [None] * 8
            hps_rr = [0]
            ag = b.sb("fl_ag", [128, NH1 * T], BF16)
            atmp = b.sb("fl_at", [128, T], F32)
            gtmp = b.sb("fl_gt", [128, T], F32)

            def w1_chunk(sh, c, slot, deps):
                wdt = 42 if c == NH1 - 1 else 128
                col0 = sh * HID + c * 128
                return b.dg(
                    w1_sb[:, slot * NCH * 128:(slot + 1) * NCH * 128]
                    .rearrange("p (a k) -> p a k", a=NCH)[:, :, 0:wdt],
                    dr["f_w1"].rearrange("l (a p) hh -> l p a hh", p=128)
                    [lidx, :, :, col0:col0 + wdt], deps=deps)

            geg = []
            prev_a = [None]
            prev_g = [None]
            order = [(sh, c) for c in range(NH1) for sh in (0, 1)]
            wtoks = [w1_chunk(0, 0, 0, ()), None]
            for idx, (sh, c) in enumerate(order):
                slot = idx % 2
                wdt = 42 if c == NH1 - 1 else 128
                pair_ps = []
                for hh in range(NT4):
                    sl = hps_rr[0] % 8
                    hps_rr[0] += 1
                    mt = None
                    for dc in range(NCH):
                        mt = b.mm(ps_h[sl][0:wdt, :],
                                  w1_sb[:, slot * NCH * 128:
                                        (slot + 1) * NCH * 128]
                                  .rearrange("p (a k) -> p a k", a=NCH)
                                  [:, dc, 0:wdt],
                                  hbf2[:, dc * T + hh * 512:
                                       dc * T + (hh + 1) * 512],
                                  start=(dc == 0), stop=(dc == NCH - 1),
                                  deps=(wtoks[slot], hps_tok[sl]))
                    pair_ps.append((sl, mt, hh))
                if idx + 1 < len(order):
                    ns, ncol = order[idx + 1]
                    wtoks[(idx + 1) % 2] = w1_chunk(
                        ns, ncol, (idx + 1) % 2,
                        (geg[-1] if geg else None,))
                if sh == 0:
                    war = geg[-1] if geg else None
                    for sl, mt, hh in pair_ps:
                        ta = b.se(lambda s, sl=sl, wdt=wdt, c=c, hh=hh:
                                  s.activation(
                                      atmp[0:wdt, hh * 512:(hh + 1) * 512],
                                      ps_h[sl][0:wdt, :], AF.Identity,
                                      bias=b1_sb[0:wdt, c:c + 1]),
                                  deps=(mt, war) + tuple(tb1))
                        hps_tok[sl] = ta
                        geg.append(ta)
                else:
                    for sl, mt, hh in pair_ps:
                        tg2 = b.se(lambda s, sl=sl, wdt=wdt, c=c, hh=hh:
                                   s.activation(
                                       gtmp[0:wdt, hh * 512:(hh + 1) * 512],
                                       ps_h[sl][0:wdt, :], AF.Gelu,
                                       bias=b1_sb[0:wdt, NH1 + c:NH1 + c + 1]),
                                   deps=(mt, prev_g[0]) + tuple(tb1))
                        hps_tok[sl] = tg2
                        tm = b.ve(lambda v, wdt=wdt, c=c, hh=hh:
                                  v.tensor_tensor(
                                      ag[0:wdt, c * T + hh * 512:
                                         c * T + (hh + 1) * 512],
                                      atmp[0:wdt, hh * 512:(hh + 1) * 512],
                                      gtmp[0:wdt, hh * 512:(hh + 1) * 512],
                                      ALU.mult), deps=(tg2,))
                        geg.append(tm)
                    prev_g[0] = geg[-1]
            b.barrier()
            w1_es.close()
            if DBG.get("tap_ag") and lidx == 0:
                prevt = None
                for c in range(NH1):
                    wdt = 42 if c == NH1 - 1 else 128
                    prevt = b.dq(tap_ag[c * 128:c * 128 + wdt, :],
                                 ag[0:wdt, c * T:(c + 1) * T], deps=(prevt,))
                b.barrier()

            w2_sb = b.sb("fl_w2", [128, 2 * NH1 * 128], BF16)
            b2_sb = b.sb("fl_b2", [128, NCH], F32)
            tb2_ = b.dq(b2_sb.ap(), dr["f_b2"][lidx].rearrange("(a p) -> p a",
                                                              p=128), slow=True)
            ps_w2 = [b.ps("fl_w2ps", [128, 512]) for _ in range(2)]
            f2 = b.sb("fl_f2", [128, 2 * 512], F32)
            x2t = b.sb("fl_x2", [128, 2 * 512], F32)
            res2 = b.sb("fl_r2", [128, 2 * 512], F32)
            w2_out = []

            def w2_chunk(e, slot, deps):
                return b.dg(
                    w2_sb[:, slot * NH1 * 128:(slot + 1) * NH1 * 128]
                    .rearrange("p (a k) -> p a k", a=NH1),
                    dr["f_w2p"].rearrange("l (a p) e -> l p a e", p=128)
                    [lidx, :, :, e * 128:(e + 1) * 128], deps=deps)

            w2toks = [w2_chunk(0, 0, ()), None]
            w2_last_mm = [None, None]
            x_src2, x_dst2 = dst_pair if dst_pair[0] is not None else (None, None)
            f_out = dst_pair[1]
            for e in range(NCH):
                slot = e % 2
                if e + 1 < NCH:
                    w2toks[(e + 1) % 2] = w2_chunk(
                        e + 1, (e + 1) % 2, (w2_last_mm[(e + 1) % 2],))
                for hh in range(NT4):
                    pst = ps_w2[hh % 2]
                    mt = None
                    for c in range(NH1):
                        wdt = 42 if c == NH1 - 1 else 128
                        mt = b.mm(pst[:],
                                  w2_sb[:, slot * NH1 * 128:
                                        (slot + 1) * NH1 * 128]
                                  .rearrange("p (a k) -> p a k", a=NH1)
                                  [0:wdt, c, :],
                                  ag[0:wdt, c * T + hh * 512:
                                     c * T + (hh + 1) * 512],
                                  start=(c == 0), stop=(c == NH1 - 1),
                                  deps=(w2toks[slot],
                                        w2_out[-2] if len(w2_out) >= 2
                                        else None))
                    w2_last_mm[slot] = mt
                    tsl = hh % 2
                    f2t = f2[:, tsl * 512:(tsl + 1) * 512]
                    x2s = x2t[:, tsl * 512:(tsl + 1) * 512]
                    r2t = res2[:, tsl * 512:(tsl + 1) * 512]
                    lf = b.dq(f2t, f_mid[e * 128:(e + 1) * 128,
                                         hh * 512:(hh + 1) * 512],
                              deps=(w2_out[-2] if len(w2_out) >= 2 else None,))
                    t1 = b.ve(lambda v, pst=pst, e=e, f2t=f2t, r2t=r2t:
                              v.scalar_tensor_tensor(
                                  r2t, pst[:], b2_sb[:, e:e + 1], f2t,
                                  ALU.add, ALU.add),
                        deps=(mt, lf, tb2_,
                              w2_out[-2] if len(w2_out) >= 2 else None))
                    if x_src2 is not None:
                        lx = b.dq(x2s, x_src2[e * 128:(e + 1) * 128,
                                              hh * 512:(hh + 1) * 512],
                                  deps=(w2_out[-2] if len(w2_out) >= 2
                                        else None,))
                        t1 = b.ve(lambda v, r2t=r2t, x2s=x2s: v.tensor_tensor(
                            r2t, r2t, x2s, ALU.add), deps=(t1, lx))
                        dst = x_dst2
                    else:
                        dst = f_out
                    w2_out.append(b.dq(
                        dst[e * 128:(e + 1) * 128, hh * 512:(hh + 1) * 512],
                        r2t, deps=(t1,)))
        fes.close()

    def ffn_call(x_src, x_dst, cut=None):
        with b.phase():
            ln_to_dram_f32(x_src, f_A)
        if cut == "ln":
            return f_A
        ffn_inner(0, f_A, f_B, (None, f_A), stop_after_c=(cut == "l0mid"))
        if cut == "l0mid":
            return f_B
        if cut == "l0ln2":
            return None
        if cut == "l0out":
            return f_A
        ffn_inner(1, f_A, f_B, (x_src, x_dst))
        return None

    def copy_out(src):
        with b.phase():
            tbuf = b.sb("cp", [128, T], F32)
            prev = None
            for c in range(NCH):
                l = b.dq(tbuf[:], src[c * 128:(c + 1) * 128, :], deps=(prev,))
                prev = b.dq(x_out[c * 128:(c + 1) * 128, :], tbuf[:], deps=(l,))

    attn_causal("a", x_in, x_A)
    if stages == 1:
        copy_out(x_A)
    if stages >= 2:
        cut = DBG.get("ffn_cut") if stages == 2 else None
        r = ffn_call(x_A, x_B if stages > 2 else x_out, cut=cut)
        if r is not None:
            copy_out(r)
    if stages >= 3:
        attn_causal("b", x_B, x_A if stages > 3 else x_out)
    if stages >= 4:
        ffn_call(x_A, x_out)
    b.barrier()

    es.close()
    return nc


def _host_prepare(inputs):
    X = np.asarray(inputs["L0_attn_a_X"], dtype=np.float32)
    Xfm = np.ascontiguousarray(X.transpose(0, 2, 1))

    def t(a):
        return np.ascontiguousarray(np.asarray(a, dtype=np.float32).T)

    qs = np.asarray(inputs["L0_ffn_q_scale"], dtype=np.float32)
    ks = np.asarray(inputs["L0_ffn_k_scale"], dtype=np.float32)
    rot = (np.arange(DH) + 32) % DH
    w2 = np.asarray(inputs["L0_ffn_w2"], dtype=np.float32)
    w2p = np.zeros((2, HIDPAD, D), dtype=np.float32)
    w2p[:, :HID, :] = w2

    qi = np.arange(128)
    m_diag = np.where(np.arange(128)[None, :] <= qi[:, None], 0.0,
                      NEG).astype(np.float32)
    kk = np.arange(128)[:, None]
    qj = np.arange(128)[None, :]
    m_low = np.where(qj >= kk, 0.0, NEG).astype(np.float32)
    m_up = np.where(qj <= kk, 0.0, NEG).astype(np.float32)
    e8 = np.zeros((8, 512), dtype=np.float32)
    for cb in range(4):
        for p in range(128):
            e8[2 * cb + p // 64, cb * 128 + p] = 1.0

    shared = {
        "aWqT": t(inputs["L0_attn_a_Wq"]), "aWkT": t(inputs["L0_attn_a_Wk"]),
        "aWvT": t(inputs["L0_attn_a_Wv"]),
        "a_bq": np.asarray(inputs["L0_attn_a_bq"], np.float32),
        "a_bk": np.asarray(inputs["L0_attn_a_bk"], np.float32),
        "a_bv": np.asarray(inputs["L0_attn_a_bv"], np.float32),
        "bWqT": t(inputs["L0_attn_b_Wq"]), "bWkT": t(inputs["L0_attn_b_Wk"]),
        "bWvT": t(inputs["L0_attn_b_Wv"]), "bWoT": t(inputs["L0_attn_b_Wo"]),
        "b_bq": np.asarray(inputs["L0_attn_b_bq"], np.float32),
        "b_bk": np.asarray(inputs["L0_attn_b_bk"], np.float32),
        "b_bv": np.asarray(inputs["L0_attn_b_bv"], np.float32),
        "b_bo": np.asarray(inputs["L0_attn_b_bo"], np.float32),
        "f_wqkv": np.ascontiguousarray(inputs["L0_ffn_wqkv"], np.float32),
        "f_wo": np.ascontiguousarray(inputs["L0_ffn_wo"], np.float32),
        "f_w1": np.ascontiguousarray(inputs["L0_ffn_w1"], np.float32),
        "f_w2p": w2p,
        "f_qs": np.ascontiguousarray(qs), "f_ks": np.ascontiguousarray(ks),
        "f_qs_rot": np.ascontiguousarray(qs[:, rot]),
        "f_ks_rot": np.ascontiguousarray(ks[:, rot]),
        "f_lng": np.ascontiguousarray(inputs["L0_ffn_ln_g"], np.float32),
        "f_lnb": np.ascontiguousarray(inputs["L0_ffn_ln_b"], np.float32),
        "f_b1": np.ascontiguousarray(inputs["L0_ffn_b1"], np.float32),
        "f_b2": np.ascontiguousarray(inputs["L0_ffn_b2"], np.float32),
        "m_diag": m_diag, "m_low": m_low, "m_up": m_up, "e8": e8,
        "m_diag_t": np.ascontiguousarray(m_diag.T),
        "m_low_t": np.ascontiguousarray(m_low.T),
        "m_up_t": np.ascontiguousarray(m_up.T),
        "ident128": np.eye(128, dtype=np.float32),
    }
    in_maps = []
    for c in range(8):
        m = dict(shared)
        m["x_fm"] = np.ascontiguousarray(Xfm[c])
        in_maps.append(m)
    return in_maps


def kernel(**inputs):
    in_maps = _host_prepare(inputs)
    nc = build_program()
    res = run_bass_kernel_spmd(nc, in_maps, core_ids=list(range(8)))
    outs = [np.asarray(r["x_outf"]) for r in res.results]
    full = np.stack(outs, axis=0)
    return np.ascontiguousarray(full.transpose(0, 2, 1))
